# version 13
# baseline (speedup 1.0000x reference)
import contextlib
import numpy as np
import concourse.bass as bass
import concourse.mybir as mybir
from concourse.bass_utils import run_bass_kernel_spmd
from concourse.alu_op_type import AluOpType as OP

F32 = mybir.dt.float32
F16 = mybir.dt.float16
I32 = mybir.dt.int32
AF = mybir.ActivationFunctionType
AX = mybir.AxisListType

NB, LSEQ, DM, DI, NST, RNK = 16, 4096, 1024, 2048, 16, 64
NEXP, NCFF, FF = 8, 2048, 4096
NROW = 8192
NCT = NCFF // 128
CHT = 512
NTT = CHT // 128
NCHB = LSEQ // CHT
NDT = DI // 128
NKD = DM // 128
NFT = FF // 128
GELU_C = 0.7978845608028654
NCORE = 8


class Prog:
    ENGS = ("sy", "gp", "v", "s", "t")

    def __init__(self, nc, st, sems):
        self.nc = nc
        self.ops = {e: [] for e in self.ENGS}
        self.sems = {}
        self.val = {}
        for name in sems:
            self.sems[name] = st.enter_context(nc.semaphore(name))
            self.val[name] = 0

    OWN = {"v": "SV", "s": "SS", "t": "ST", "gp": "SG"}

    def emit(self, eng, fn, waits=(), inc=None, inc_n=1, sd=False):
        waits = tuple(waits)
        if sd and eng in self.OWN and self.OWN[eng] in self.val:
            waits = waits + ((self.OWN[eng], self.val[self.OWN[eng]]),)
        if inc is not None:
            self.val[inc] += inc_n
        self.ops[eng].append((fn, waits, inc, inc_n))
        return self.val[inc] if inc is not None else None

    def install(self, block):
        def mk(key):
            def body(eng):
                for fn, waits, inc, inc_n in self.ops[key]:
                    seen = {}
                    for s, v in waits:
                        if v and seen.get(s, -1) < v:
                            seen[s] = v
                    for s, v in seen.items():
                        eng.wait_ge(self.sems[s], v)
                    instr = fn(eng)
                    if inc is not None:
                        if instr is None:
                            eng.sem_inc(self.sems[inc], inc_n)
                        else:
                            instr.then_inc(self.sems[inc], inc_n)
            return body
        block.sync(mk("sy"))
        block.gpsimd(mk("gp"))
        block.vector(mk("v"))
        block.scalar(mk("s"))
        block.tensor(mk("t"))


def build_mega(nht=0, K2=0, n_chunks=2 * NCHB, startup_limit=None, debug=False):
    nc = bass.Bass("TRN2", target_bir_lowering=False)
    xin = nc.dram_tensor("xin", [2, LSEQ, DM], F32, kind="ExternalInput").ap()
    win_t = nc.dram_tensor("win_t", [DM, 2 * DI], F32, kind="ExternalInput").ap()
    wout_t = nc.dram_tensor("wout_t", [DI, DM], F32, kind="ExternalInput").ap()
    xpw_t = nc.dram_tensor("xpw_t", [128, NDT, 96], F32, kind="ExternalInput").ap()
    wdt_t = nc.dram_tensor("wdt_t", [RNK, DI], F32, kind="ExternalInput").ap()
    convw = nc.dram_tensor("convw", [128, NDT, 4], F32, kind="ExternalInput").ap()
    convb = nc.dram_tensor("convb", [128, NDT], F32, kind="ExternalInput").ap()
    alog = nc.dram_tensor("alog", [128, NDT, NST], F32, kind="ExternalInput").ap()
    dpar = nc.dram_tensor("dpar", [128, NDT], F32, kind="ExternalInput").ap()
    dtbv = nc.dram_tensor("dtbv", [128, NDT], F32, kind="ExternalInput").ap()
    sels = nc.dram_tensor("sels", [NST, NST * 128], F16, kind="ExternalInput").ap()
    ident = nc.dram_tensor("ident", [128, 128], F16, kind="ExternalInput").ap()
    resid = nc.dram_tensor("resid", [2, LSEQ, DM], F32, kind="ExternalOutput").ap()
    winf = nc.dram_tensor("winf", [DM, 2 * DI], F16, kind="Internal").ap()
    if debug:
        dbg_wf = nc.dram_tensor("dbg_wf", [DI, DM], F16, kind="ExternalOutput").ap()
        dbg_xct = nc.dram_tensor("dbg_xct", [128, NDT, CHT], F16, kind="ExternalOutput").ap()
        dbg_delta = nc.dram_tensor("dbg_delta", [128, NDT, CHT], F16, kind="ExternalOutput").ap()
        dbg_xdbl = nc.dram_tensor("dbg_xdbl", [128, NTT, 96], F16, kind="ExternalOutput").ap()
        dbg_brep = nc.dram_tensor("dbg_brep", [128, NST, CHT], F16, kind="ExternalOutput").ap()
        dbg_y = nc.dram_tensor("dbg_y", [128, NDT, CHT], F16, kind="ExternalOutput").ap()
        dbg_lnxT = nc.dram_tensor("dbg_lnxT", [128, NKD, CHT], F16, kind="ExternalOutput").ap()
    woutf = nc.dram_tensor("woutf", [DI, DM], F16, kind="Internal").ap()
    GW = 192 + nht * (K2 + 1)
    w1f = nc.dram_tensor("w1f", [NEXP, NFT, 128, NKD, 128], F32, kind="ExternalInput").ap()
    w2f = nc.dram_tensor("w2f", [NEXP, 2, FF, 512], F32, kind="ExternalInput").ap()
    b1e = nc.dram_tensor("b1e", [NEXP, 128, NFT], F32, kind="ExternalInput").ap()
    b2e = nc.dram_tensor("b2e", [NEXP, 128, DM], F32, kind="ExternalInput").ap()
    toka = nc.dram_tensor("toka", [NEXP, 128, NCT], I32, kind="ExternalInput").ap()
    cwa = nc.dram_tensor("cwa", [NROW, NEXP], F32, kind="ExternalInput").ap()
    gidx = nc.dram_tensor("gidx", [128, GW], I32, kind="ExternalInput").ap()
    w1H = nc.dram_tensor("w1H", [NEXP, NFT, 128, NKD, 128], F16, kind="Internal").ap()
    w2H = nc.dram_tensor("w2H", [NEXP, 2, FF, 512], F16, kind="Internal").ap()
    wgtH = nc.dram_tensor("wgtH", [NEXP * NCFF + 1, DM], F16, kind="Internal").ap()
    final01 = nc.dram_tensor("final01", [NROW, DM], F32, kind="ExternalOutput").ap()
    if debug:
        dbgW = nc.dram_tensor("dbgW", [NEXP * NCFF, DM], F16, kind="ExternalOutput").ap()

    st = contextlib.ExitStack()
    def sb(name, shape, dt):
        return st.enter_context(nc.sbuf_tensor(name, shape, dt))[:]
    def psm(name, shape, dt):
        return st.enter_context(nc.psum_tensor(name, shape, dt))[:]
    with st:
        wxp = sb("wxp", [128, NDT, 96], F16)
        wdt = sb("wdt", [RNK, DI], F16)
        cw = sb("cw", [128, NDT, 4], F32)
        cb = sb("cb", [128, NDT], F32)
        nega = sb("nega", [128, NDT, NST], F32)
        alog_s = sb("alog_s", [128, NDT, NST], F32)
        dp_s = sb("dp_s", [128, NDT], F32)
        dtb_s = sb("dtb_s", [128, NDT], F32)
        sels_s = sb("sels_s", [NST, NST * 128], F16)
        id_s = sb("id_s", [128, 128], F16)
        winsl = sb("winsl", [128, 4, NKD, 128], F16)
        woutsl = sb("woutsl", [128, 2, 512], F16)
        xbuf = sb("xbuf", [128, NTT, DM], F32)
        lnx = sb("lnx", [128, 1, DM], F16)
        lnxT = sb("lnxT", [128, NKD, CHT], F16)
        xtb = sb("xtb", [128, NDT, CHT + 3], F16)
        zsil = sb("zsil", [128, NDT, CHT], F16)
        xct = sb("xct", [128, NDT, CHT], F16)
        delta = sb("delta", [128, NDT, CHT], F16)
        cacc = sb("cacc", [128, CHT], F16)
        xdbl = sb("xdbl", [128, NTT, 96], F16)
        dtlT = sb("dtlT", [RNK, CHT], F16)
        btile = sb("btile", [NST, CHT], F16)
        ctile = sb("ctile", [NST, CHT], F16)
        brep = sb("brep", [128, NST, CHT], F16)
        crep = sb("crep", [128, NST, CHT], F16)
        lns = sb("lns", [128, 12], F32)
        spt = sb("spt", [128, CHT], F32)
        spe = spt
        xr = sb("xr", [128, 3, 512], F32)
        ut = sb("ut", [128, CHT], F16)
        aslt = sb("aslt", [128, 2, CHT], F32)
        ubs = sb("ubs", [128, CHT], F16)
        hbig = sb("hbig", [128, NST, CHT], F16)
        ytmp = sb("ytmp", [128, CHT], F16)
        yall = sb("yall", [128, NDT, CHT], F16)
        hc = sb("hc", [128, 2, NDT * NST], F32)
        rsl = sb("rsl", [128, 2, 512], F32)
        tok_s = sb("tok_s", [128, NCT], I32)
        wcmb = sb("wcmb", [128, 2, NCT, NEXP], F32)
        b1s = sb("b1s", [128, 2, NFT], F32)
        ghx = sb("ghx", [128, max(GW, 4)], I32)
        sqt = rsl.rearrange("p a b -> p (a b)").unsqueeze(1)
        pA = psm("pA", [128, 2, 512], F32)
        pB = psm("pB", [128, 2, 512], F32)
        pC = psm("pC", [128, 2, 512], F32)
        pTf = psm("pTf", [128, 2, 1024], F16)
        pT = pTf[:, :, 0:512]
        pS = pTf[0:RNK, 0, 0:512]
        pBC = pTf[0:NST, :, 0:512]

        P = Prog(nc, st, ("SX", "SR", "SG", "SV", "SS", "ST", "XL", "A0", "A1",
                          "O0", "O1", "U0", "U1", "U2", "U3",
                          "W0", "W1", "W2b", "W3", "B0", "B1",
                          "R0", "R1", "R2", "R3", "WF0", "WF1",
                          "XR0", "XR1", "XR2", "CA", "TK",
                          "GA0", "GA1", "GB0", "GB1", "GC0", "GC1",
                          "XA0", "XA1", "HR", "HG0", "HG1", "CP", "CL0", "CL1"))
        _E = P.emit
        _cnt = [0]
        def E(*a, **kw):
            _cnt[0] += 1
            if startup_limit is not None and _cnt[0] > startup_limit:
                return P.val.get(kw.get("inc"), 0) if kw.get("inc") else None
            return _E(*a, **kw)
        def W(*engs):
            m = {"x": "SX", "r": "SR", "g": "SG", "v": "SV", "s": "SS", "t": "ST"}
            return [(m[e], P.val[m[e]]) for e in engs]

        xbufV0 = xbuf.rearrange("p a b -> p (a b)")
        zsilV0 = zsil.rearrange("p a b -> p (a b)")
        cst = []
        for i in range(8):
            for c in range(2):
                cst.append((win_t[i * 128:(i + 1) * 128, c * 2048:(c + 1) * 2048],
                            winf[i * 128:(i + 1) * 128, c * 2048:(c + 1) * 2048], 2048))
        for i in range(16):
            cst.append((wout_t[i * 128:(i + 1) * 128, :],
                        woutf[i * 128:(i + 1) * 128, :], 1024))
        Mw = {}
        for i, (srcap, dstap, wd) in enumerate(cst):
            sl = i % 2
            lw = [("CP", Mw[("cp", sl)])] if ("cp", sl) in Mw else []
            mld = E("sy", lambda sy, srcap=srcap, sl=sl, wd=wd: sy.dma_start(
                out=xbufV0[:, sl * 2048:sl * 2048 + wd], in_=srcap),
                waits=lw, inc="CL%d" % sl, inc_n=16)
            E("gp", lambda g, sl=sl, wd=wd: g.tensor_copy(
                out=zsilV0[:, sl * 2048:sl * 2048 + wd],
                in_=xbufV0[:, sl * 2048:sl * 2048 + wd]),
                waits=[("CL%d" % sl, mld), ("WF%d" % sl, Mw.get(("st", sl), 0))],
                inc="CP")
            Mw[("cp", sl)] = P.val["CP"]
            E("sy", lambda sy, dstap=dstap, sl=sl, wd=wd: sy.dma_start(
                out=dstap, in_=zsilV0[:, sl * 2048:sl * 2048 + wd]),
                waits=[("CP", Mw[("cp", sl)])], inc="WF%d" % sl, inc_n=16)
            Mw[("st", sl)] = P.val["WF%d" % sl]
        wf_tot = [("WF0", P.val["WF0"]), ("WF1", P.val["WF1"])]
        for fn in (
            lambda g: g.dma_start(out=wxp, in_=xpw_t),
            lambda g: g.dma_start(out=wdt, in_=wdt_t),
            lambda g: g.dma_start(out=cw, in_=convw),
            lambda g: g.dma_start(out=cb, in_=convb),
            lambda g: g.dma_start(out=alog_s, in_=alog),
            lambda g: g.dma_start(out=dp_s, in_=dpar),
            lambda g: g.dma_start(out=dtb_s, in_=dtbv),
            lambda g: g.dma_start(out=sels_s, in_=sels),
            lambda g: g.dma_start(out=id_s, in_=ident),
        ):
            E("gp", fn, inc="SG", inc_n=16)
        m_wload = P.val["SG"]
        m_cast = []
        for e in range(NEXP):
            cw_ = [("CA", m_cast[-1])] if m_cast else []
            E("gp", lambda g, e=e: g.dma_start(out=w1H[e], in_=w1f[e]),
              waits=cw_, inc="CA", inc_n=16)
            E("gp", lambda g, e=e: g.dma_start(out=w2H[e], in_=w2f[e]),
              inc="CA", inc_n=16)
            m_cast.append(P.val["CA"])
        E("s", lambda s: s.activation(out=nega, in_=alog_s, func=AF.Exp),
          waits=W("g"), inc="SS")
        E("v", lambda v: v.tensor_scalar(
            out=nega, in0=nega, scalar1=-1.0, scalar2=None, op0=OP.mult),
            waits=W("s"), inc="SV", sd=True)

        M = {}
        m_tts = {}
        for g in range(n_chunks):
            b, ci = divmod(g, NCHB)
            t0 = ci * CHT

            m_x = E("sy", lambda sy, b=b, t0=t0: sy.dma_start(
                out=xbuf, in_=xin[b, t0:t0 + CHT, :].rearrange("(tt p) d -> p tt d", p=128)),
                waits=[("SV", M.get("xfree", 0)), ("SG", m_wload if g == 0 else 0)]
                      + (wf_tot if g == 0 else []),
                inc="XL", inc_n=16)

            for tt in range(NTT):
                E("v", lambda v, tt=tt: v.tensor_reduce(
                    out=lns[:, tt:tt + 1], in_=xbuf[:, tt, :], axis=AX.X, op=OP.add),
                    waits=[("XL", m_x)], inc="SV")
                E("s", lambda s, tt=tt: s.activation(
                    out=sqt[:, 0, :], in_=xbuf[:, tt, :], func=AF.Square),
                    waits=[("XL", m_x), ("SV", P.val["SV"]), ("SR", P.val["SR"])],
                    inc="SS", sd=True)
                E("v", lambda v, tt=tt: v.tensor_reduce(
                    out=lns[:, 4 + tt:5 + tt], in_=sqt[:, 0, :], axis=AX.X, op=OP.add),
                    waits=[("SS", P.val["SS"])], inc="SV")
            E("v", lambda v: v.tensor_scalar(
                out=lns[:, 0:8], in0=lns[:, 0:8], scalar1=1.0 / DM, scalar2=None,
                op0=OP.mult), inc="SV", sd=True)
            E("v", lambda v: v.tensor_tensor(
                out=lns[:, 8:12], in0=lns[:, 0:4], in1=lns[:, 0:4], op=OP.mult),
                inc="SV", sd=True)
            E("v", lambda v: v.scalar_tensor_tensor(
                out=lns[:, 4:8], in0=lns[:, 4:8], scalar=1e-5, in1=lns[:, 8:12],
                op0=OP.add, op1=OP.subtract), inc="SV", sd=True)
            E("s", lambda s: s.activation(out=lns[:, 4:8], in_=lns[:, 4:8], func=AF.Ln),
              waits=[("SV", P.val["SV"])], inc="SS", sd=True)
            m_rstd = E("s", lambda s: s.activation(
                out=lns[:, 4:8], in_=lns[:, 4:8], func=AF.Exp, scale=-0.5),
                inc="SS", sd=True)

            for tt in range(NTT):
                E("v", lambda v, tt=tt: v.tensor_scalar(
                    out=lnx[:, 0, :], in0=xbuf[:, tt, :],
                    scalar1=lns[:, tt:tt + 1], scalar2=lns[:, 4 + tt:5 + tt],
                    op0=OP.subtract, op1=OP.mult),
                    waits=[("SS", m_rstd), ("ST", M.get("tp_last", 0))], inc="SV")
                m_ts = P.val["SV"]
                if tt == NTT - 1:
                    M["xfree"] = m_ts
                for ds in range(NKD):
                    par = (tt * NKD + ds) % 2
                    E("t", lambda t, ds=ds, par=par: t.transpose(
                        pT[:, par, 0:128], lnx[:, 0, ds * 128:(ds + 1) * 128], id_s),
                        waits=[("SV", m_ts), ("SV", M.get(("ptev", par), 0))], inc="ST")
                    M["tp_last"] = P.val["ST"]
                    E("v", lambda v, ds=ds, tt=tt, par=par: v.tensor_copy(
                        out=lnxT[:, ds, tt * 128:(tt + 1) * 128], in_=pT[:, par, 0:128]),
                        waits=[("ST", P.val["ST"])], inc="SV")
                    M[("ptev", par)] = P.val["SV"]
            m_lnxT = P.val["SV"]

            if ci == 0:
                E("v", lambda v: v.memset(xtb[:, :, 0:3], 0.0), inc="SV")
            else:
                E("v", lambda v: v.tensor_copy(
                    out=xtb[:, :, 0:3], in_=xtb[:, :, CHT:CHT + 3]), inc="SV")

            for o in range(2 * NDT):
                slot = o % 2
                wld = [("ST", M[("ip_mm", slot)])] if ("ip_mm", slot) in M else []
                mw = E("sy", lambda g_, o=o, slot=slot: g_.dma_start(
                    out=winsl[:, slot],
                    in_=winf[:, o * 128:(o + 1) * 128].rearrange("(kd p) o -> p kd o", p=128)),
                    waits=wld + ([("SG", m_wload)] + wf_tot if g == 0 and o < 2 else []),
                    inc="A%d" % slot, inc_n=16)
                for k in range(NKD):
                    w2 = []
                    if k == 0:
                        w2 = [("A%d" % slot, mw), ("SV", m_lnxT)]
                        if ("ip_ev", slot) in M:
                            w2.append(M[("ip_ev", slot)])
                        if g > 0 and o == 0:
                            w2.append(("SV", M["radd_last"]))
                    E("t", lambda t, k=k, slot=slot: t.matmul(
                        pA[:, slot, :], winsl[:, slot, k, :], lnxT[:, k, :],
                        start=(k == 0), stop=(k == NKD - 1)),
                        waits=w2, inc="ST" if k == NKD - 1 else None)
                M[("ip_mm", slot)] = P.val["ST"]
                if o < NDT:
                    E("v", lambda v, o=o, slot=slot: v.tensor_copy(
                        out=xtb[:, o, 3:CHT + 3], in_=pA[:, slot, :]),
                        waits=[("ST", M[("ip_mm", slot)])], inc="SV")
                    M[("ip_ev", slot)] = ("SV", P.val["SV"])
                else:
                    w3 = [("ST", M[("ip_mm", slot)])]
                    if g > 0 and o == NDT:
                        w3.append(("SV", M["yall_last"]))
                    E("s", lambda s, o=o, slot=slot: s.activation(
                        out=zsil[:, o - NDT, :], in_=pA[:, slot, :], func=AF.Silu),
                        waits=w3, inc="SS")
                    M[("ip_ev", slot)] = ("SS", P.val["SS"])

            for dt in range(NDT):
                w = [("SS", M[("silu", dt - 1)])] if ("silu", dt - 1) in M else \
                    ([("SS", M[("silu", NDT - 1)])] if ("silu", NDT - 1) in M else [])
                E("v", lambda v, dt=dt: v.tensor_scalar(
                    out=cacc, in0=xtb[:, dt, 0:CHT], scalar1=cw[:, dt, 0:1],
                    scalar2=None, op0=OP.mult), waits=w, inc="SV", sd=True)
                for k in range(1, 4):
                    E("v", lambda v, dt=dt, k=k: v.scalar_tensor_tensor(
                        out=cacc, in0=xtb[:, dt, k:CHT + k], scalar=cw[:, dt, k:k + 1],
                        in1=cacc, op0=OP.mult, op1=OP.add), inc="SV", sd=True)
                w = [("SV", P.val["SV"])]
                if g > 0 and dt == 0:
                    w.append(("SV", M["yall_last"]))
                E("s", lambda s, dt=dt: s.activation(
                    out=xct[:, dt, :], in_=cacc, func=AF.Silu, bias=cb[:, dt:dt + 1]),
                    waits=w, inc="SS")
                M[("silu", dt)] = P.val["SS"]
            m_silu_all = P.val["SS"]

            for tt in range(NTT):
                par = tt % 2
                for kd in range(NDT):
                    w = []
                    if kd == 0:
                        w = [("SS", m_silu_all)]
                        if ("xp_ev", par) in M:
                            w.append(("SV", M[("xp_ev", par)]))
                        elif g > 0:
                            w.append(("SV", M["radd_last"]))
                    E("t", lambda t, tt=tt, kd=kd, par=par: t.matmul(
                        pB[:, par, 0:96], xct[:, kd, tt * 128:(tt + 1) * 128], wxp[:, kd, :],
                        start=(kd == 0), stop=(kd == NDT - 1)),
                        waits=w, inc="ST" if kd == NDT - 1 else None)
                E("v", lambda v, tt=tt, par=par: v.tensor_copy(
                    out=xdbl[:, tt, :], in_=pB[:, par, 0:96]),
                    waits=[("ST", P.val["ST"])], inc="SV")
                M[("xp_ev", par)] = P.val["SV"]
            m_xdbl = P.val["SV"]

            for tt in range(NTT):
                E("t", lambda t, tt=tt: t.transpose(
                    pS[:, tt * 128:(tt + 1) * 128], xdbl[:, tt, 0:RNK], id_s),
                    waits=[("SV", m_xdbl), ("SV", M.get(("ptev", 0), 0))] if tt == 0 else [],
                    inc="ST" if tt == NTT - 1 else None)
            E("v", lambda v: v.tensor_copy(out=dtlT, in_=pS),
              waits=[("ST", P.val["ST"])], inc="SV")
            M[("ptev", 0)] = P.val["SV"]
            for tt in range(NTT):
                E("t", lambda t, tt=tt: t.transpose(
                    pBC[:, 1, tt * 128:(tt + 1) * 128], xdbl[:, tt, RNK:RNK + NST], id_s),
                    waits=[("SV", M.get(("ptev", 1), 0))] if tt == 0 else [],
                    inc="ST" if tt == NTT - 1 else None)
            E("v", lambda v: v.tensor_copy(out=btile, in_=pBC[:, 1, :]),
              waits=[("ST", P.val["ST"])], inc="SV")
            M[("ptev", 1)] = P.val["SV"]
            for tt in range(NTT):
                E("t", lambda t, tt=tt: t.transpose(
                    pBC[:, 0, tt * 128:(tt + 1) * 128], xdbl[:, tt, RNK + NST:96], id_s),
                    waits=[("SV", M.get(("ptev", 0), 0))] if tt == 0 else [],
                    inc="ST" if tt == NTT - 1 else None)
            E("v", lambda v: v.tensor_copy(out=ctile, in_=pBC[:, 0, :]),
              waits=[("ST", P.val["ST"])], inc="SV")
            M[("ptev", 0)] = P.val["SV"]
            m_bc = P.val["SV"]

            for n in range(NST):
                par = n % 2
                w = [("SV", m_bc)] if n == 0 else []
                if n <= 1:
                    w.append(M[("ip_ev", par)])
                if ("rp_ev", par, 0) in M and n >= 2:
                    w.append(("SV", M[("rp_ev", par, 0)]))
                E("t", lambda t, n=n, par=par: t.matmul(
                    pA[:, par, :], sels_s[:, n * 128:(n + 1) * 128], btile,
                    start=True, stop=True), waits=w, inc="ST")
                mb = P.val["ST"]
                w = []
                if n >= 2 and ("rp_ev", par, 1) in M:
                    w.append(("SV", M[("rp_ev", par, 1)]))
                E("t", lambda t, n=n, par=par: t.matmul(
                    pB[:, par, :], sels_s[:, n * 128:(n + 1) * 128], ctile,
                    start=True, stop=True), waits=w, inc="ST")
                mc = P.val["ST"]
                E("v", lambda v, n=n, par=par: v.tensor_copy(
                    out=brep[:, n, :], in_=pA[:, par, :]), waits=[("ST", mb)], inc="SV")
                M[("rp_ev", par, 0)] = P.val["SV"]
                E("v", lambda v, n=n, par=par: v.tensor_copy(
                    out=crep[:, n, :], in_=pB[:, par, :]), waits=[("ST", mc)], inc="SV")
                M[("rp_ev", par, 1)] = P.val["SV"]

            for dt in range(NDT):
                par = dt % 2
                w = [("SV", M[("rp_ev", par, 0)])] if dt <= 1 else [("SV", M[("dt_ev", par)])]
                E("t", lambda t, dt=dt, par=par: t.matmul(
                    pA[:, par, :], wdt[:, dt * 128:(dt + 1) * 128], dtlT,
                    start=True, stop=True), waits=w, inc="ST")
                wv = [("ST", P.val["ST"])]
                if dt:
                    wv.append(("SS", P.val["SS"]))
                elif g > 0:
                    wv.append(("SV", M["ut_last"]))
                E("v", lambda v, dt=dt, par=par: v.tensor_scalar(
                    out=spt, in0=pA[:, par, :], scalar1=dtb_s[:, dt:dt + 1], scalar2=30.0,
                    op0=OP.add, op1=OP.min), waits=wv, inc="SV", sd=True)
                M[("dt_ev", par)] = P.val["SV"]
                E("s", lambda s: s.activation(out=spe, in_=spt, func=AF.Exp),
                  waits=[("SV", P.val["SV"])], inc="SS", sd=True)
                wl = []
                if g > 0 and dt == 0:
                    wl.append(("SV", M["ut_last"]))
                E("s", lambda s, dt=dt: s.activation(
                    out=delta[:, dt, :], in_=spe, func=AF.Ln, bias=1.0),
                    waits=wl, inc="SS", sd=True)
            m_delta = P.val["SS"]

            if ci == 0:
                E("v", lambda v, g=g: v.memset(hc[:, g % 2, :], 0.0), inc="SV")
            for dt in range(NDT):
                E("v", lambda v, dt=dt: v.tensor_tensor(
                    out=ut, in0=delta[:, dt, :], in1=xct[:, dt, :], op=OP.mult),
                    waits=[("SS", P.val["SS"] if dt else m_delta)], inc="SV", sd=True)
                if dt == NDT - 1:
                    M["ut_last"] = P.val["SV"]
                for n in range(NST):
                    sl = n % 2
                    key = g * 256 + dt * NST + n
                    w = []
                    if key >= 2 and (key - 2) in m_tts:
                        w = [("SV", m_tts[key - 2])]
                    E("s", lambda s, dt=dt, n=n, sl=sl: s.activation(
                        out=aslt[:, sl, :], in_=delta[:, dt, :], func=AF.Exp,
                        scale=nega[:, dt, n:n + 1]), waits=w, inc="SS")
                    m_a = P.val["SS"]
                    E("v", lambda v, n=n: v.tensor_tensor(
                        out=ubs, in0=ut, in1=brep[:, n, :], op=OP.mult), inc="SV", sd=True)
                    m_tts[key] = E("v", lambda v, n=n, sl=sl, g=g, dt=dt: v.tensor_tensor_scan(
                        out=hbig[:, n, :], data0=aslt[:, sl, :], data1=ubs,
                        initial=hc[:, g % 2, dt * NST + n:dt * NST + n + 1],
                        op0=OP.mult, op1=OP.add),
                        waits=[("SS", m_a)], inc="SV", sd=True)
                E("v", lambda v, dt=dt, g=g: v.tensor_copy(
                    out=hc[:, (g + 1) % 2, dt * NST:(dt + 1) * NST],
                    in_=hbig[:, :, CHT - 1]), inc="SV", sd=True)
                E("v", lambda v: v.tensor_tensor(
                    out=hbig, in0=hbig, in1=crep, op=OP.mult), inc="SV", sd=True)
                E("v", lambda v: v.tensor_tensor(
                    out=hbig[:, 0:8, :], in0=hbig[:, 0:8, :], in1=hbig[:, 8:16, :],
                    op=OP.add), inc="SV", sd=True)
                E("v", lambda v: v.tensor_tensor(
                    out=hbig[:, 0:4, :], in0=hbig[:, 0:4, :], in1=hbig[:, 4:8, :],
                    op=OP.add), inc="SV", sd=True)
                E("v", lambda v: v.tensor_tensor(
                    out=hbig[:, 0:2, :], in0=hbig[:, 0:2, :], in1=hbig[:, 2:4, :],
                    op=OP.add), inc="SV", sd=True)
                E("v", lambda v: v.tensor_tensor(
                    out=ytmp, in0=hbig[:, 0, :], in1=hbig[:, 1, :], op=OP.add),
                    inc="SV", sd=True)
                E("v", lambda v, dt=dt: v.scalar_tensor_tensor(
                    out=ytmp, in0=xct[:, dt, :], scalar=dp_s[:, dt:dt + 1], in1=ytmp,
                    op0=OP.mult, op1=OP.add), inc="SV", sd=True)
                w = [("ST", M.get(("wo_k", 1, dt), 0))]
                if dt == 0:
                    w.append(M[("ip_ev", 1)])
                E("v", lambda v, dt=dt: v.tensor_tensor(
                    out=yall[:, dt, :], in0=ytmp, in1=zsil[:, dt, :], op=OP.mult),
                    waits=w, inc="SV", sd=True)
                M[("gate", dt)] = P.val["SV"]
            M["yall_last"] = P.val["SV"]

            if debug and g == 0:
                E("sy", lambda sy: sy.dma_start(out=dbg_wf, in_=woutf),
                  waits=W("g"), inc="SR", inc_n=16)
                for nm, tns in (("dbg_xct", xct), ("dbg_delta", delta), ("dbg_xdbl", xdbl),
                                ("dbg_brep", brep), ("dbg_y", yall), ("dbg_lnxT", lnxT)):
                    E("sy", lambda sy, nm=nm, tns=tns: sy.dma_start(
                        out={"dbg_xct": dbg_xct, "dbg_delta": dbg_delta,
                             "dbg_xdbl": dbg_xdbl, "dbg_brep": dbg_brep,
                             "dbg_y": dbg_y, "dbg_lnxT": dbg_lnxT}[nm], in_=tns),
                        waits=W("v", "s"), inc="SR", inc_n=16)

            for fh in range(2):
                for k in range(NDT):
                    slot = k % 2
                    wld = [("ST", M[("wo_mm", slot)])] if ("wo_mm", slot) in M else []
                    mwo = E("sy", lambda g_, k=k, fh=fh, slot=slot: g_.dma_start(
                        out=woutsl[:, slot, :],
                        in_=woutf[k * 128:(k + 1) * 128, fh * 512:(fh + 1) * 512]),
                        waits=wld + (wf_tot if g == 0 and fh == 0 and k < 2 else []),
                        inc="O%d" % slot, inc_n=16)
                    for tt in range(NTT):
                        pp = pA if tt < 2 else pB
                        w2 = [("O%d" % slot, mwo)] if tt == 0 else []
                        if tt == 0 and fh == 0:
                            w2.append(("SV", M[("gate", k)]))
                        if k == 0:
                            if fh == 0:
                                w2.append(("SV", M[("dt_ev", tt % 2)] if tt < 2
                                           else M[("rp_ev", tt % 2, 1)]))
                            else:
                                w2.append(("SV", M[("radd", fh - 1, tt)]))
                        E("t", lambda t, k=k, tt=tt, slot=slot, pp=pp: t.matmul(
                            pp[:, tt % 2, :], yall[:, k, tt * 128:(tt + 1) * 128],
                            woutsl[:, slot, :],
                            start=(k == 0), stop=(k == NDT - 1)),
                            waits=w2, inc="ST" if tt == NTT - 1 else None)
                    M[("wo_mm", slot)] = P.val["ST"]
                    M[("wo_k", fh, k)] = P.val["ST"]
                m_opmm = P.val["ST"]
                for tt in range(NTT):
                    pp = pA if tt < 2 else pB
                    sl = (fh * NTT + tt) % 3
                    i = fh * NTT + tt
                    wxr = [("SV", M[("radd", (i - 3) // NTT, (i - 3) % NTT)])] if i >= 3 \
                        else ([("SV", M.get(("radd", 1, i + 1), 0))] if g > 0 else [])
                    mxr = E("sy", lambda sy, b=b, t0=t0, tt=tt, fh=fh, sl=sl: sy.dma_start(
                        out=xr[:, sl, :],
                        in_=xin[b, t0 + tt * 128:t0 + (tt + 1) * 128,
                                fh * 512:(fh + 1) * 512]),
                        waits=wxr, inc="XR%d" % sl, inc_n=16)
                    E("v", lambda v, tt=tt, fh=fh, pp=pp, sl=sl, mxr=mxr: v.tensor_tensor(
                        out=rsl[:, tt % 2, :], in0=pp[:, tt % 2, :], in1=xr[:, sl, :],
                        op=OP.add),
                        waits=[("ST", m_opmm), ("XR%d" % sl, mxr), ("SR", P.val["SR"])],
                        inc="SV")
                    M[("radd", fh, tt)] = P.val["SV"]
                    E("sy", lambda sy, b=b, t0=t0, tt=tt, fh=fh: sy.dma_start(
                        out=resid[b, t0 + tt * 128:t0 + (tt + 1) * 128,
                                  fh * 512:(fh + 1) * 512], in_=rsl[:, tt % 2, :]),
                        waits=[("SV", P.val["SV"])], inc="SR", inc_n=16)
            M["radd_last"] = P.val["SV"]

        m_resid_done = P.val["SR"]
        m_scan_end = M["radd_last"]
        residR = resid.rearrange("b l d -> (b l) d")
        xbufV = xbuf.rearrange("p a b -> p (a b)")
        zsilV = zsil.rearrange("p a b -> p (a b)")
        xctV = xct.rearrange("p a b -> p (a b)")
        deltaV = delta.rearrange("p a b -> p (a b)")
        brepV = brep.rearrange("p a b -> p (a b)")
        crepV = crep.rearrange("p a b -> p (a b)")
        lnxTV = lnxT.rearrange("p a b -> p (a b)")
        w2p = [ubs, ut, cacc, ytmp]

        def lxg_ap(ctl, lo, hi):
            base = (ctl % 8) * DM
            return (xctV if ctl < 8 else deltaV)[:, base + lo:base + hi]

        def xgT_ap(k, c0, c1):
            base = (k % 4) * 2048
            return (brepV if k < 4 else crepV)[:, base + c0:base + c1]

        def hq_ap(f, t0, t1):
            return (hbig if f < 16 else yall)[:, f % 16, t0:t1]

        Mf = {}
        E("sy", lambda sy: sy.dma_start(out=ghx[:, 0:GW], in_=gidx),
          inc="SX", inc_n=16)
        m_ghx = P.val["SX"]

        def ffn_params(e):
            E("sy", lambda sy, e=e: sy.dma_start(out=b1s[:, e % 2, :], in_=b1e[e]),
              waits=[("SV", Mf["gelu_ts_last"])] if e >= 2 else [],
              inc="B%d" % (e % 2), inc_n=16)
            bw = [("SV", Mf["w2evac_last"])] if e >= 2 else [("SV", M["xfree"])]
            E("sy", lambda sy, e=e: sy.dma_start(
                out=xbuf[:, 2 + e % 2, :], in_=b2e[e]), waits=bw,
                inc="B%d" % (e % 2), inc_n=16)
            Mf[("bb", e % 2)] = P.val["B%d" % (e % 2)]

        def ffn_gather_ln(e):
            wt = [("TK", P.val["TK"])]
            if "g_last" not in Mf:
                wt.append(("SR", m_resid_done))
            m_tok = E("gp", lambda g, e=e: g.dma_start(out=tok_s, in_=toka[e]),
                      waits=wt, inc="TK", inc_n=16)
            for ctl in range(NCT):
                sl = ctl % 2
                gw = [("TK", m_tok)]
                if e == 0 and ctl < 2:
                    gw += [("SR", m_resid_done), ("SV", M["xfree"])]
                if ("lnr", (ctl - 2) % NCT) in Mf and (e > 0 or ctl >= 2):
                    gw.append(("SV", Mf[("lnr", (ctl - 2) % NCT)]))
                m_g = E("gp", lambda g, ctl=ctl, sl=sl: g.indirect_dma_start(
                    out=xbufV[:, sl * DM:(sl + 1) * DM], out_offset=None,
                    in_=residR[0:128, :],
                    in_offset=bass.IndirectOffsetOnAxis(ap=tok_s[:, ctl:ctl + 1], axis=0)),
                    waits=gw, inc="GA%d" % sl, inc_n=16)
                cww = []
                if ctl == 0 and e >= 2:
                    cww = [("SV", Mf["w2evac_last"])]
                E("gp", lambda g, e=e, ctl=ctl: g.indirect_dma_start(
                    out=wcmb[:, e % 2, ctl, :], out_offset=None, in_=cwa[0:128, :],
                    in_offset=bass.IndirectOffsetOnAxis(ap=tok_s[:, ctl:ctl + 1], axis=0)),
                    waits=cww, inc="GB%d" % (e % 2), inc_n=16)
                Mf["g_last"] = True
                Mf[("wcmb", e % 2)] = P.val["GB%d" % (e % 2)]
                xg = xbufV[:, sl * DM:(sl + 1) * DM]
                E("v", lambda v, xg=xg: v.tensor_reduce(
                    out=lns[:, 0:1], in_=xg, axis=AX.X, op=OP.add),
                    waits=[("GA%d" % sl, m_g)], inc="SV")
                sqw = [("GA%d" % sl, m_g), ("SV", P.val["SV"]), ("SS", P.val["SS"])]
                if e == 0 and ctl == 0:
                    sqw.append(("SR", m_resid_done))
                E("s", lambda s, xg=xg: s.activation(out=sqt[:, 0, :], in_=xg,
                                                     func=AF.Square),
                  waits=sqw, inc="SS")
                E("v", lambda v: v.tensor_reduce(
                    out=lns[:, 1:2], in_=sqt[:, 0, :], axis=AX.X, op=OP.add),
                    waits=[("SS", P.val["SS"])], inc="SV")
                E("v", lambda v: v.tensor_scalar(
                    out=lns[:, 0:2], in0=lns[:, 0:2], scalar1=1.0 / DM, scalar2=None,
                    op0=OP.mult), inc="SV", sd=True)
                E("v", lambda v: v.tensor_tensor(
                    out=lns[:, 2:3], in0=lns[:, 0:1], in1=lns[:, 0:1], op=OP.mult),
                    inc="SV", sd=True)
                E("v", lambda v: v.scalar_tensor_tensor(
                    out=lns[:, 1:2], in0=lns[:, 1:2], scalar=1e-5, in1=lns[:, 2:3],
                    op0=OP.add, op1=OP.subtract), inc="SV", sd=True)
                E("s", lambda s: s.activation(out=lns[:, 1:2], in_=lns[:, 1:2], func=AF.Ln),
                  waits=[("SV", P.val["SV"])], inc="SS", sd=True)
                E("s", lambda s: s.activation(out=lns[:, 1:2], in_=lns[:, 1:2],
                                              func=AF.Exp, scale=-0.5), inc="SS", sd=True)
                lxw = [("SS", P.val["SS"])]
                if ctl == 0 and "tp_last" in Mf:
                    lxw.append(("ST", Mf["tp_last"]))
                E("v", lambda v, ctl=ctl, xg=xg: v.tensor_scalar(
                    out=lxg_ap(ctl, 0, DM), in0=xg, scalar1=lns[:, 0:1],
                    scalar2=lns[:, 1:2], op0=OP.subtract, op1=OP.mult),
                    waits=lxw, inc="SV")
                Mf[("lnr", ctl)] = P.val["SV"]
            Mf["lxg"] = P.val["SV"]

        def ffn_transpose(e):
            m_lxg = Mf["lxg"]
            for ds in range(NKD):
                for cq in range(4):
                    par = (ds * 4 + cq) % 2
                    for j in range(4):
                        ctl = cq * 4 + j
                        wts = []
                        if ds == 0 and cq == 0 and j == 0:
                            wts = [("SV", m_lxg), ("ST", Mf.get("w1_last", 0))]
                        if j == 0:
                            wts.append(("SV", M.get(("ptev", par), 0)))
                        E("t", lambda t, ds=ds, ctl=ctl, j=j, par=par: t.transpose(
                            pT[:, par, j * 128:(j + 1) * 128],
                            lxg_ap(ctl, ds * 128, (ds + 1) * 128), id_s),
                            waits=wts, inc="ST" if j == 3 else None)
                    E("v", lambda v, ds=ds, cq=cq, par=par: v.tensor_copy(
                        out=xgT_ap(ds, cq * 512, (cq + 1) * 512), in_=pT[:, par, :]),
                        waits=[("ST", P.val["ST"])], inc="SV")
                    M[("ptev", par)] = P.val["SV"]
            Mf["xgT"] = P.val["SV"]
            Mf["tp_last"] = P.val["ST"]

        def ffn_w1w2(e):
            m_xgT = Mf["xgT"]
            for q in range(4):
                for f in range(NFT):
                    ws = f % 4
                    ps = f % 2
                    lw = []
                    if q == 0 and f < 4:
                        lw.append(("CA", m_cast[e]))
                    if ("w1_mm", ws) in Mf:
                        lw.append(("ST", Mf[("w1_mm", ws)]))
                    elif e == 0 and ws < 2:
                        lw.append(("ST", M[("ip_mm", ws)]))
                    mwl = E("sy", lambda sy, e=e, f=f, ws=ws: sy.dma_start(
                        out=winsl[:, ws], in_=w1H[e, f]),
                        waits=lw, inc="U%d" % ws, inc_n=16)
                    for k in range(NKD):
                        w_ = []
                        if k == 0:
                            w_ = [("U%d" % ws, mwl)]
                            if ("gelu_evac", ps) in Mf:
                                w_.append(("SV", Mf[("gelu_evac", ps)]))
                            if f < 2 and q == 0:
                                w_.append(("SV", m_xgT))
                        E("t", lambda t, q=q, f=f, k=k, ws=ws, ps=ps: t.matmul(
                            pC[:, ps, :], winsl[:, ws, k, :],
                            xgT_ap(k, q * 512, (q + 1) * 512),
                            start=(k == 0), stop=(k == NKD - 1)),
                            waits=w_, inc="ST" if k == NKD - 1 else None)
                    Mf[("w1_mm", ws)] = P.val["ST"]
                    gx = zsilV[:, ps * 512:(ps + 1) * 512]
                    gsq = zsilV[:, (2 + ps) * 512:(3 + ps) * 512]
                    gx3 = zsilV[:, (4 + ps) * 512:(5 + ps) * 512]
                    gth = zsilV[:, (6 + ps) * 512:(7 + ps) * 512]
                    gw1 = [("ST", Mf[("w1_mm", ws)])]
                    if q == 0 and f == 0:
                        gw1.append(("B%d" % (e % 2), Mf[("bb", e % 2)]))
                    if ("g_sg", ps) in Mf:
                        gw1.append(("SS", Mf[("g_sg", ps)]))
                    E("v", lambda v, e=e, f=f, ps=ps, gx=gx: v.tensor_scalar(
                        out=gx, in0=pC[:, ps, :], scalar1=b1s[:, e % 2, f:f + 1],
                        scalar2=None, op0=OP.add), waits=gw1, inc="SV")
                    Mf[("gelu_evac", ps)] = P.val["SV"]
                    Mf["gelu_ts_last"] = P.val["SV"]
                    E("s", lambda s, gx=gx, gsq=gsq: s.activation(
                        out=gsq, in_=gx, func=AF.Square),
                        waits=[("SV", P.val["SV"])], inc="SS")
                    E("v", lambda v, gx=gx, gsq=gsq, gx3=gx3: v.tensor_tensor(
                        out=gx3, in0=gsq, in1=gx, op=OP.mult),
                        waits=[("SS", P.val["SS"])], inc="SV")
                    E("v", lambda v, gx=gx, gx3=gx3: v.scalar_tensor_tensor(
                        out=gx3, in0=gx3, scalar=0.044715, in1=gx,
                        op0=OP.mult, op1=OP.add), inc="SV", sd=True)
                    E("s", lambda s, gx3=gx3, gth=gth: s.activation(
                        out=gth, in_=gx3, func=AF.Sigmoid, scale=2.0 * GELU_C),
                        waits=[("SV", P.val["SV"])], inc="SS")
                    Mf[("g_sg", ps)] = P.val["SS"]
                    hw = [("SS", P.val["SS"])]
                    if "w2_mm_last" in Mf:
                        hw.append(("ST", Mf["w2_mm_last"]))
                    E("v", lambda v, f=f, gx=gx, gth=gth: v.tensor_tensor(
                        out=hq_ap(f, 0, 512), in0=gth, in1=gx, op=OP.mult),
                        waits=hw, inc="SV", sd=True)
                    Mf[("g6", f)] = P.val["SV"]
                if q == 3:
                    Mf["w1_last"] = P.val["ST"]
                    if e + 1 < NEXP:
                        ffn_transpose(e + 1)
                for dh in range(2):
                    for k in range(NFT):
                        ks = k % 4
                        piece = w2p[ks]
                        pw = []
                        if q == 0 and dh == 0 and k < 4:
                            pw.append(("CA", m_cast[e]))
                        if ("w2_ld", ks) in Mf:
                            pw.append(("ST", Mf[("w2_ld", ks)]))
                        elif e == 0:
                            pw.append(("SV", m_scan_end))
                        wsem = ("W0", "W1", "W2b", "W3")[ks]
                        mw2 = E("sy", lambda sy, e=e, dh=dh, k=k, piece=piece: sy.dma_start(
                            out=piece, in_=w2H[e, dh, k * 128:(k + 1) * 128, :]),
                            waits=pw, inc=wsem, inc_n=16)
                        for tt in range(NTT):
                            pp = pA if tt < 2 else pB
                            w_ = [(wsem, mw2)] if tt == 0 else []
                            if tt == 0 and dh == 0:
                                w_.append(("SV", Mf[("g6", k)]))
                            if k == 0:
                                if dh == 0:
                                    if "w2evac_last" in Mf:
                                        w_.append(("SV", Mf["w2evac_last"]))
                                    else:
                                        w_.append(("SV", m_scan_end))
                                else:
                                    w_.append(("SV", Mf["w2evac_dh0"]))
                            E("t", lambda t, k=k, tt=tt, piece=piece, pp=pp: t.matmul(
                                pp[:, tt % 2, :], hq_ap(k, tt * 128, (tt + 1) * 128),
                                piece, start=(k == 0), stop=(k == NFT - 1)),
                                waits=w_, inc="ST" if tt == NTT - 1 else None)
                        Mf[("w2_ld", ks)] = P.val["ST"]
                    Mf["w2_mm_last"] = P.val["ST"]
                    m_pass = P.val["ST"]
                    for tt in range(NTT):
                        ct = q * 4 + tt
                        pp = pA if tt < 2 else pB
                        wv = [("ST", m_pass)]
                        if dh == 0 and tt == 0:
                            wv.append(("B%d" % (e % 2), Mf[("bb", e % 2)]))
                            wv.append(("GB%d" % (e % 2), Mf[("wcmb", e % 2)]))
                        if ("wgt_st", tt) in Mf:
                            wv.append(("R%d" % tt, Mf[("wgt_st", tt)]))
                        dst = lnxTV[:, tt * 1024 + dh * 512:tt * 1024 + dh * 512 + 512]
                        b2rr = xbufV[:, (2 + e % 2) * DM + dh * 512:
                                     (2 + e % 2) * DM + dh * 512 + 512]
                        E("v", lambda v, pp=pp, tt=tt, dst=dst, b2rr=b2rr: v.tensor_tensor(
                            out=dst, in0=pp[:, tt % 2, :], in1=b2rr, op=OP.add),
                            waits=wv, inc="SV")
                        E("v", lambda v, e=e, ct=ct, dst=dst: v.tensor_scalar(
                            out=dst, in0=dst, scalar1=wcmb[:, e % 2, ct, e:e + 1],
                            scalar2=None, op0=OP.mult), inc="SV", sd=True)
                    if dh == 0:
                        Mf["w2evac_dh0"] = P.val["SV"]
                    else:
                        Mf["w2evac_last"] = P.val["SV"]
                        for tt in range(NTT):
                            ct = q * 4 + tt
                            E("sy", lambda sy, e=e, ct=ct, tt=tt: sy.dma_start(
                                out=wgtH[e * NCFF + ct * 128:e * NCFF + (ct + 1) * 128, :],
                                in_=lnxTV[:, tt * 1024:(tt + 1) * 1024]),
                                waits=[("SV", P.val["SV"])], inc="R%d" % tt, inc_n=16)
                            Mf[("wgt_st", tt)] = P.val["R%d" % tt]

        ffn_params(0)
        ffn_gather_ln(0)
        ffn_transpose(0)
        for e in range(NEXP):
            if e + 1 < NEXP:
                ffn_params(e + 1)
                ffn_gather_ln(e + 1)
            ffn_w1w2(e)

        m_wgt_all = P.val["SR"]
        wgt_tot = [("R%d" % t, P.val["R%d" % t]) for t in range(4)]
        if debug:
            E("sy", lambda sy: sy.dma_start(out=dbgW, in_=wgtH[0:NEXP * NCFF, :]),
              waits=[("SR", m_wgt_all)] + wgt_tot, inc="SR", inc_n=16)
            m_wgt_all = P.val["SR"]
        E("v", lambda v: v.memset(lnx[:, 0, :], 0.0), inc="SV", sd=True)
        E("sy", lambda sy: sy.dma_start(
            out=wgtH[NEXP * NCFF:NEXP * NCFF + 1, :], in_=lnx[0:1, 0, :]),
            waits=[("SV", P.val["SV"]), ("SR", m_resid_done)], inc="SR", inc_n=16)
        m_zrow = P.val["SR"]
        for rt in range(NROW // 128):
            sl = rt % 2
            rw = [("SV", Mf[("cadd", rt - 2)])] if ("cadd", rt - 2) in Mf \
                else [("SV", P.val["SV"]), ("SS", P.val["SS"])]
            acc = xbufV[:, sl * DM:(sl + 1) * DM]
            m_r = E("sy", lambda sy, rt=rt, acc=acc: sy.dma_start(
                out=acc, in_=residR[rt * 128:(rt + 1) * 128, :]),
                waits=rw, inc="XA%d" % sl, inc_n=16)
            for j in range(3):
                gws = []
                if j == 0:
                    gws = [("SR", m_zrow), ("SX", m_ghx)]
                    if rt == 0:
                        gws += [("SR", m_wgt_all), ("ST", Mf["w2_mm_last"]),
                                ("SS", P.val["SS"]), ("SV", P.val["SV"])] + wgt_tot
                    if ("cadd", rt - 2) in Mf:
                        gws.append(("SV", Mf[("cadd", rt - 2)]))
                E("gp", lambda g, rt=rt, sl=sl, j=j: g.indirect_dma_start(
                    out=zsilV[:, (sl * 3 + j) * DM:(sl * 3 + j + 1) * DM],
                    out_offset=None, in_=wgtH[0:128, :],
                    in_offset=bass.IndirectOffsetOnAxis(
                        ap=ghx[:, j * 64 + rt:j * 64 + rt + 1], axis=0)),
                    waits=gws, inc="GC%d" % sl, inc_n=16)
            m_gg = P.val["GC%d" % sl]
            for j in range(3):
                aw = [("XA%d" % sl, m_r), ("GC%d" % sl, m_gg)] if j == 0 else []
                E("v", lambda v, sl=sl, j=j, acc=acc: v.tensor_tensor(
                    out=acc, in0=acc,
                    in1=zsilV[:, (sl * 3 + j) * DM:(sl * 3 + j + 1) * DM], op=OP.add),
                    waits=aw, inc="SV", sd=True)
            Mf[("cadd", rt)] = P.val["SV"]
            E("sy", lambda sy, rt=rt, acc=acc: sy.dma_start(
                out=final01[rt * 128:(rt + 1) * 128, :], in_=acc),
                waits=[("SV", P.val["SV"]), ("SR", m_zrow)], inc="SR", inc_n=16)
        m_light = P.val["SR"]
        hacc = xbufV[:, 3 * DM:4 * DM]
        m_hsc = 0
        for ht in range(nht):
            base = 192 + ht * (K2 + 1)
            hw0 = [("SR", m_hsc)] if m_hsc else [("SV", P.val["SV"])]
            m_hr = E("gp", lambda g, base=base, hacc=hacc: g.indirect_dma_start(
                out=hacc, out_offset=None, in_=residR[0:128, :],
                in_offset=bass.IndirectOffsetOnAxis(ap=ghx[:, base:base + 1], axis=0)),
                waits=hw0, inc="HR", inc_n=16)
            for j in range(K2):
                sl2 = 6 + (j % 2)
                if ("hadd", j - 2) in Mf:
                    gw2 = [("SV", Mf[("hadd", j - 2)])]
                else:
                    gw2 = [("SV", P.val["SV"])]
                m_hg = E("gp", lambda g, base=base, j=j, sl2=sl2: g.indirect_dma_start(
                    out=zsilV[:, sl2 * DM:(sl2 + 1) * DM], out_offset=None,
                    in_=wgtH[0:128, :],
                    in_offset=bass.IndirectOffsetOnAxis(
                        ap=ghx[:, base + 1 + j:base + 2 + j], axis=0)),
                    waits=gw2, inc="HG%d" % (j % 2), inc_n=16)
                aw = [("HG%d" % (j % 2), m_hg)] + ([("HR", m_hr)] if j == 0 else [])
                E("v", lambda v, sl2=sl2, hacc=hacc: v.tensor_tensor(
                    out=hacc, in0=hacc,
                    in1=zsilV[:, sl2 * DM:(sl2 + 1) * DM], op=OP.add),
                    waits=aw, inc="SV", sd=True)
                Mf[("hadd", j)] = P.val["SV"]
            for j in range(K2):
                Mf.pop(("hadd", j), None)
            E("gp", lambda g, base=base, hacc=hacc: g.indirect_dma_start(
                out=final01, out_offset=bass.IndirectOffsetOnAxis(
                    ap=ghx[:, base:base + 1], axis=0),
                in_=hacc, in_offset=None),
                waits=[("SV", P.val["SV"]), ("SR", m_light)], inc="SR", inc_n=16)
            m_hsc = P.val["SR"]

        E("sy", lambda sy: None, waits=W("r"))

        with nc.Block() as block:
            P.install(block)
    return nc


def kernel(**inputs):
    hs = np.ascontiguousarray(inputs["L0_attn_hidden_states"], dtype=np.float32)
    def tile_d(a):
        a = np.asarray(a, dtype=np.float32)
        return np.ascontiguousarray(a.reshape(NDT, 128, *a.shape[1:]).swapaxes(0, 1))
    win_t = np.ascontiguousarray(inputs["L0_attn_in_proj_w"].T, dtype=np.float32)
    convw = tile_d(inputs["L0_attn_conv_w"][:, 0, :])
    convb = tile_d(inputs["L0_attn_conv_b"])
    xpw_t = tile_d(np.asarray(inputs["L0_attn_x_proj_w"], dtype=np.float32).T)
    wdt_t = np.ascontiguousarray(inputs["L0_attn_dt_proj_w"].T, dtype=np.float32)
    dtbv = tile_d(inputs["L0_attn_dt_proj_b"])
    alog = tile_d(inputs["L0_attn_A_log"])
    dpar = tile_d(inputs["L0_attn_D_param"])
    wout_t = np.ascontiguousarray(inputs["L0_attn_out_proj_w"].T, dtype=np.float32)
    cmbw = np.ascontiguousarray(inputs["L0_ffn_combine_weights"], dtype=np.float32)
    topi = np.ascontiguousarray(inputs["L0_ffn_top_idx"], dtype=np.int64)
    w1a = np.ascontiguousarray(inputs["L0_ffn_W1"], dtype=np.float32)
    b1a = np.ascontiguousarray(inputs["L0_ffn_b1"], dtype=np.float32)
    w2a = np.ascontiguousarray(inputs["L0_ffn_W2"], dtype=np.float32)
    b2a = np.ascontiguousarray(inputs["L0_ffn_b2"], dtype=np.float32)

    sels = np.zeros((NST, NST * 128), dtype=np.float16)
    for n in range(NST):
        sels[n, n * 128:(n + 1) * 128] = 1.0
    ident = np.eye(128, dtype=np.float16)

    DUMMY = NEXP * NCFF
    tgt = np.ascontiguousarray(topi.T).reshape(-1)
    cnt = np.bincount(tgt, minlength=NROW)
    ordf = np.argsort(tgt, kind="stable")
    starts = np.zeros(NROW + 1, np.int64)
    starts[1:] = np.cumsum(cnt)
    L = 3
    g3 = np.full((NROW, L), DUMMY, np.int64)
    light = cnt <= L
    for j in range(L):
        m = light & (cnt > j)
        g3[m, j] = ordf[starts[:-1][m] + j]
    heavy = np.where(cnt > L)[0]
    nh = len(heavy)
    if nh:
        K2 = int(cnt[heavy].max())
        nht = (nh + 127) // 128
        npad = nht * 128 - nh
        pads = np.where(light)[0][:npad]
        hrows = np.concatenate([heavy, pads])
        H = np.full((nht * 128, K2), DUMMY, np.int64)
        for i, r in enumerate(hrows):
            H[i, :cnt[r]] = ordf[starts[r]:starts[r] + cnt[r]]
    else:
        K2, nht = 0, 0
    GW = 192 + nht * (K2 + 1)
    gidx = np.zeros((128, GW), np.int32)
    for rt in range(NROW // 128):
        for j in range(L):
            gidx[:, j * 64 + rt] = g3[rt * 128:(rt + 1) * 128, j]
    for ht in range(nht):
        base = 192 + ht * (K2 + 1)
        gidx[:, base] = hrows[ht * 128:(ht + 1) * 128]
        for j in range(K2):
            gidx[:, base + 1 + j] = H[ht * 128:(ht + 1) * 128, j]

    w1t = np.ascontiguousarray(
        w1a.reshape(NEXP, NKD, 128, NFT, 128).transpose(0, 3, 2, 1, 4))
    w2t = np.ascontiguousarray(w2a.reshape(NEXP, FF, 2, 512).transpose(0, 2, 1, 3))
    b1t = np.ascontiguousarray(b1a.reshape(NEXP, NFT, 128).transpose(0, 2, 1))
    b2t = np.ascontiguousarray(np.broadcast_to(b2a[:, None, :], (NEXP, 128, DM)))
    tokt = np.ascontiguousarray(
        np.ascontiguousarray(topi.T).reshape(NEXP, NCT, 128).transpose(0, 2, 1)
        .astype(np.int32))
    cwt = cmbw

    nc1 = build_mega(nht=nht, K2=K2)
    maps = []
    for c in range(NCORE):
        maps.append(dict(
            xin=np.ascontiguousarray(hs[2 * c:2 * c + 2]),
            win_t=win_t, wout_t=wout_t, xpw_t=xpw_t, wdt_t=wdt_t,
            convw=convw, convb=convb, alog=alog, dpar=dpar, dtbv=dtbv,
            sels=sels, ident=ident,
            w1f=w1t, w2f=w2t, b1e=b1t, b2e=b2t, toka=tokt, cwa=cwt, gidx=gidx,
        ))
    r1 = run_bass_kernel_spmd(nc1, maps, core_ids=list(range(NCORE)))

    out = np.empty((NB, LSEQ, DM), dtype=np.float32)
    out[0:2] = np.asarray(r1.results[0]["final01"]).reshape(2, LSEQ, DM)
    for c in range(1, NCORE):
        out[2 * c:2 * c + 2] = np.asarray(r1.results[c]["resid"])
    return out


# version 14
# speedup vs baseline: 1.2184x; 1.2184x over previous
import contextlib
import numpy as np
import concourse.bass as bass
import concourse.mybir as mybir
from concourse.bass_utils import run_bass_kernel_spmd
from concourse.alu_op_type import AluOpType as OP

F32 = mybir.dt.float32
F16 = mybir.dt.float16
I32 = mybir.dt.int32
AF = mybir.ActivationFunctionType
AX = mybir.AxisListType

NB, LSEQ, DM, DI, NST, RNK = 16, 4096, 1024, 2048, 16, 64
NEXP, NCFF, FF = 8, 2048, 4096
NROW = 8192
NCT = NCFF // 128
CHT = 512
NTT = CHT // 128
NCHB = LSEQ // CHT
NDT = DI // 128
NKD = DM // 128
NFT = FF // 128
GELU_C = 0.7978845608028654
NCORE = 8


class Prog:
    ENGS = ("sy", "gp", "v", "s", "t")

    def __init__(self, nc, st, sems):
        self.nc = nc
        self.ops = {e: [] for e in self.ENGS}
        self.sems = {}
        self.val = {}
        for name in sems:
            self.sems[name] = st.enter_context(nc.semaphore(name))
            self.val[name] = 0

    OWN = {"v": "SV", "s": "SS", "t": "ST", "gp": "SG"}

    def emit(self, eng, fn, waits=(), inc=None, inc_n=1, sd=False):
        waits = tuple(waits)
        if sd and eng in self.OWN and self.OWN[eng] in self.val:
            waits = waits + ((self.OWN[eng], self.val[self.OWN[eng]]),)
        if inc is not None:
            self.val[inc] += inc_n
        self.ops[eng].append((fn, waits, inc, inc_n))
        return self.val[inc] if inc is not None else None

    def install(self, block):
        def mk(key):
            def body(eng):
                for fn, waits, inc, inc_n in self.ops[key]:
                    seen = {}
                    for s, v in waits:
                        if v and seen.get(s, -1) < v:
                            seen[s] = v
                    for s, v in seen.items():
                        eng.wait_ge(self.sems[s], v)
                    instr = fn(eng)
                    if inc is not None:
                        if instr is None:
                            eng.sem_inc(self.sems[inc], inc_n)
                        else:
                            instr.then_inc(self.sems[inc], inc_n)
            return body
        block.sync(mk("sy"))
        block.gpsimd(mk("gp"))
        block.vector(mk("v"))
        block.scalar(mk("s"))
        block.tensor(mk("t"))


def build_mega(nht=0, K2=0, n_chunks=2 * NCHB, startup_limit=None, debug=False):
    nc = bass.Bass("TRN2", target_bir_lowering=False)
    xin = nc.dram_tensor("xin", [2, LSEQ, DM], F32, kind="ExternalInput").ap()
    win_t = nc.dram_tensor("win_t", [DM, 2 * DI], F32, kind="ExternalInput").ap()
    wout_t = nc.dram_tensor("wout_t", [DI, DM], F32, kind="ExternalInput").ap()
    xpw_t = nc.dram_tensor("xpw_t", [128, NDT, 96], F32, kind="ExternalInput").ap()
    wdt_t = nc.dram_tensor("wdt_t", [RNK, DI], F32, kind="ExternalInput").ap()
    convw = nc.dram_tensor("convw", [128, NDT, 4], F32, kind="ExternalInput").ap()
    convb = nc.dram_tensor("convb", [128, NDT], F32, kind="ExternalInput").ap()
    alog = nc.dram_tensor("alog", [128, NDT, NST], F32, kind="ExternalInput").ap()
    dpar = nc.dram_tensor("dpar", [128, NDT], F32, kind="ExternalInput").ap()
    dtbv = nc.dram_tensor("dtbv", [128, NDT], F32, kind="ExternalInput").ap()
    sels = nc.dram_tensor("sels", [NST, NST * 128], F16, kind="ExternalInput").ap()
    ident = nc.dram_tensor("ident", [128, 128], F16, kind="ExternalInput").ap()
    resid = nc.dram_tensor("resid", [2, LSEQ, DM], F32, kind="ExternalOutput").ap()
    winf = nc.dram_tensor("winf", [DM, 2 * DI], F16, kind="Internal").ap()
    if debug:
        dbg_wf = nc.dram_tensor("dbg_wf", [DI, DM], F16, kind="ExternalOutput").ap()
        dbg_xct = nc.dram_tensor("dbg_xct", [128, NDT, CHT], F16, kind="ExternalOutput").ap()
        dbg_delta = nc.dram_tensor("dbg_delta", [128, NDT, CHT], F16, kind="ExternalOutput").ap()
        dbg_xdbl = nc.dram_tensor("dbg_xdbl", [128, NTT, 96], F16, kind="ExternalOutput").ap()
        dbg_brep = nc.dram_tensor("dbg_brep", [128, NST, CHT], F16, kind="ExternalOutput").ap()
        dbg_y = nc.dram_tensor("dbg_y", [128, NDT, CHT], F16, kind="ExternalOutput").ap()
        dbg_lnxT = nc.dram_tensor("dbg_lnxT", [128, NKD, CHT], F16, kind="ExternalOutput").ap()
    woutf = nc.dram_tensor("woutf", [DI, DM], F16, kind="Internal").ap()
    GW = 192 + nht * (K2 + 1)
    w1f = nc.dram_tensor("w1f", [NEXP, NFT, 128, NKD, 128], F32, kind="ExternalInput").ap()
    w2f = nc.dram_tensor("w2f", [NEXP, 2, FF, 512], F32, kind="ExternalInput").ap()
    b1e = nc.dram_tensor("b1e", [NEXP, 128, NFT], F32, kind="ExternalInput").ap()
    b2e = nc.dram_tensor("b2e", [NEXP, 128, DM], F32, kind="ExternalInput").ap()
    toka = nc.dram_tensor("toka", [NEXP, 128, NCT], I32, kind="ExternalInput").ap()
    cwa = nc.dram_tensor("cwa", [NROW, NEXP], F32, kind="ExternalInput").ap()
    gidx = nc.dram_tensor("gidx", [128, GW], I32, kind="ExternalInput").ap()
    w1H = nc.dram_tensor("w1H", [NEXP, NFT, 128, NKD, 128], F16, kind="Internal").ap()
    w2H = nc.dram_tensor("w2H", [NEXP, 2, FF, 512], F16, kind="Internal").ap()
    wgtH = nc.dram_tensor("wgtH", [NEXP * NCFF + 1, DM], F16, kind="Internal").ap()
    final01 = nc.dram_tensor("final01", [NROW, DM], F32, kind="ExternalOutput").ap()
    if debug:
        dbgW = nc.dram_tensor("dbgW", [NEXP * NCFF, DM], F16, kind="ExternalOutput").ap()

    st = contextlib.ExitStack()
    def sb(name, shape, dt):
        return st.enter_context(nc.sbuf_tensor(name, shape, dt))[:]
    def psm(name, shape, dt):
        return st.enter_context(nc.psum_tensor(name, shape, dt))[:]
    with st:
        wxp = sb("wxp", [128, NDT, 96], F16)
        wdt = sb("wdt", [RNK, DI], F16)
        cw = sb("cw", [128, NDT, 4], F32)
        cb = sb("cb", [128, NDT], F32)
        nega = sb("nega", [128, NDT, NST], F32)
        alog_s = sb("alog_s", [128, NDT, NST], F32)
        dp_s = sb("dp_s", [128, NDT], F32)
        dtb_s = sb("dtb_s", [128, NDT], F32)
        sels_s = sb("sels_s", [NST, NST * 128], F16)
        id_s = sb("id_s", [128, 128], F16)
        winsl = sb("winsl", [128, 4, NKD, 128], F16)
        woutsl = sb("woutsl", [128, 2, 512], F16)
        xbuf = sb("xbuf", [128, NTT, DM], F32)
        lnx = sb("lnx", [128, 1, DM], F16)
        lnxT = sb("lnxT", [128, NKD, CHT], F16)
        xtb = sb("xtb", [128, NDT, CHT + 3], F16)
        zsil = sb("zsil", [128, NDT, CHT], F16)
        xct = sb("xct", [128, NDT, CHT], F16)
        delta = sb("delta", [128, NDT, CHT], F16)
        cacc = sb("cacc", [128, CHT], F16)
        xdbl = sb("xdbl", [128, NTT, 96], F16)
        dtlT = sb("dtlT", [RNK, CHT], F16)
        btile = sb("btile", [NST, CHT], F16)
        ctile = sb("ctile", [NST, CHT], F16)
        brep = sb("brep", [128, NST, CHT], F16)
        crep = sb("crep", [128, NST, CHT], F16)
        lns = sb("lns", [128, 12], F32)
        spt = sb("spt", [128, CHT], F32)
        spe = spt
        xr = sb("xr", [128, 3, 512], F32)
        ut = sb("ut", [128, CHT], F16)
        aslt = sb("aslt", [128, 2, CHT], F32)
        ubs = sb("ubs", [128, CHT], F16)
        hbig = sb("hbig", [128, NST, CHT], F16)
        ytmp = sb("ytmp", [128, CHT], F16)
        yall = sb("yall", [128, NDT, CHT], F16)
        hc = sb("hc", [128, 2, NDT * NST], F32)
        rsl = sb("rsl", [128, 2, 512], F32)
        tok_s = sb("tok_s", [128, NCT], I32)
        wcmb = sb("wcmb", [128, 2, NCT, NEXP], F32)
        b1s = sb("b1s", [128, 2, NFT], F32)
        ghx = sb("ghx", [128, max(GW, 4)], I32)
        sqt = rsl.rearrange("p a b -> p (a b)").unsqueeze(1)
        pA = psm("pA", [128, 2, 512], F32)
        pB = psm("pB", [128, 2, 512], F32)
        pC = psm("pC", [128, 2, 512], F32)
        pTf = psm("pTf", [128, 2, 1024], F16)
        pT = pTf[:, :, 0:512]
        pS = pTf[0:RNK, 0, 0:512]
        pBC = pTf[0:NST, :, 0:512]

        P = Prog(nc, st, ("SX", "SR", "SG", "SV", "SS", "ST", "XL", "A0", "A1",
                          "O0", "O1", "U0", "U1", "U2", "U3",
                          "W0", "W1", "W2b", "W3", "B0", "B1",
                          "R0", "R1", "R2", "R3", "WF0", "WF1",
                          "XR0", "XR1", "XR2", "CA", "TK",
                          "GA0", "GA1", "GB0", "GB1", "GC0", "GC1",
                          "XA0", "XA1", "HR", "HG0", "HG1", "CP", "CL0", "CL1"))
        _E = P.emit
        _cnt = [0]
        def E(*a, **kw):
            _cnt[0] += 1
            if startup_limit is not None and _cnt[0] > startup_limit:
                return P.val.get(kw.get("inc"), 0) if kw.get("inc") else None
            return _E(*a, **kw)
        def W(*engs):
            m = {"x": "SX", "r": "SR", "g": "SG", "v": "SV", "s": "SS", "t": "ST"}
            return [(m[e], P.val[m[e]]) for e in engs]

        xbufV0 = xbuf.rearrange("p a b -> p (a b)")
        zsilV0 = zsil.rearrange("p a b -> p (a b)")
        cst = []
        for i in range(8):
            for c in range(2):
                cst.append((win_t[i * 128:(i + 1) * 128, c * 2048:(c + 1) * 2048],
                            winf[i * 128:(i + 1) * 128, c * 2048:(c + 1) * 2048], 2048))
        for i in range(16):
            cst.append((wout_t[i * 128:(i + 1) * 128, :],
                        woutf[i * 128:(i + 1) * 128, :], 1024))
        Mw = {}
        for i, (srcap, dstap, wd) in enumerate(cst):
            sl = i % 2
            lw = [("CP", Mw[("cp", sl)])] if ("cp", sl) in Mw else []
            mld = E("sy", lambda sy, srcap=srcap, sl=sl, wd=wd: sy.dma_start(
                out=xbufV0[:, sl * 2048:sl * 2048 + wd], in_=srcap),
                waits=lw, inc="CL%d" % sl, inc_n=16)
            E("gp", lambda g, sl=sl, wd=wd: g.tensor_copy(
                out=zsilV0[:, sl * 2048:sl * 2048 + wd],
                in_=xbufV0[:, sl * 2048:sl * 2048 + wd]),
                waits=[("CL%d" % sl, mld), ("WF%d" % sl, Mw.get(("st", sl), 0))],
                inc="CP")
            Mw[("cp", sl)] = P.val["CP"]
            E("sy", lambda sy, dstap=dstap, sl=sl, wd=wd: sy.dma_start(
                out=dstap, in_=zsilV0[:, sl * 2048:sl * 2048 + wd]),
                waits=[("CP", Mw[("cp", sl)])], inc="WF%d" % sl, inc_n=16)
            Mw[("st", sl)] = P.val["WF%d" % sl]
        wf_tot = [("WF0", P.val["WF0"]), ("WF1", P.val["WF1"])]
        for fn in (
            lambda g: g.dma_start(out=wxp, in_=xpw_t),
            lambda g: g.dma_start(out=wdt, in_=wdt_t),
            lambda g: g.dma_start(out=cw, in_=convw),
            lambda g: g.dma_start(out=cb, in_=convb),
            lambda g: g.dma_start(out=alog_s, in_=alog),
            lambda g: g.dma_start(out=dp_s, in_=dpar),
            lambda g: g.dma_start(out=dtb_s, in_=dtbv),
            lambda g: g.dma_start(out=sels_s, in_=sels),
            lambda g: g.dma_start(out=id_s, in_=ident),
        ):
            E("gp", fn, inc="SG", inc_n=16)
        m_wload = P.val["SG"]
        m_cast = []
        for e in range(NEXP):
            cw_ = [("CA", m_cast[-1])] if m_cast else []
            E("gp", lambda g, e=e: g.dma_start(out=w1H[e], in_=w1f[e]),
              waits=cw_, inc="CA", inc_n=16)
            E("gp", lambda g, e=e: g.dma_start(out=w2H[e], in_=w2f[e]),
              inc="CA", inc_n=16)
            m_cast.append(P.val["CA"])
        E("s", lambda s: s.activation(out=nega, in_=alog_s, func=AF.Exp),
          waits=W("g"), inc="SS")
        E("v", lambda v: v.tensor_scalar(
            out=nega, in0=nega, scalar1=-1.0, scalar2=None, op0=OP.mult),
            waits=W("s"), inc="SV", sd=True)

        M = {}
        m_tts = {}
        for g in range(n_chunks):
            b, ci = divmod(g, NCHB)
            t0 = ci * CHT

            m_x = E("sy", lambda sy, b=b, t0=t0: sy.dma_start(
                out=xbuf, in_=xin[b, t0:t0 + CHT, :].rearrange("(tt p) d -> p tt d", p=128)),
                waits=[("SV", M.get("xfree", 0)), ("SG", m_wload if g == 0 else 0)]
                      + (wf_tot if g == 0 else []),
                inc="XL", inc_n=16)

            for tt in range(NTT):
                E("v", lambda v, tt=tt: v.tensor_reduce(
                    out=lns[:, tt:tt + 1], in_=xbuf[:, tt, :], axis=AX.X, op=OP.add),
                    waits=[("XL", m_x)], inc="SV")
                E("s", lambda s, tt=tt: s.activation(
                    out=sqt[:, 0, :], in_=xbuf[:, tt, :], func=AF.Square),
                    waits=[("XL", m_x), ("SV", P.val["SV"]), ("SR", P.val["SR"])],
                    inc="SS", sd=True)
                E("v", lambda v, tt=tt: v.tensor_reduce(
                    out=lns[:, 4 + tt:5 + tt], in_=sqt[:, 0, :], axis=AX.X, op=OP.add),
                    waits=[("SS", P.val["SS"])], inc="SV")
            E("v", lambda v: v.tensor_scalar(
                out=lns[:, 0:8], in0=lns[:, 0:8], scalar1=1.0 / DM, scalar2=None,
                op0=OP.mult), inc="SV", sd=True)
            E("v", lambda v: v.tensor_tensor(
                out=lns[:, 8:12], in0=lns[:, 0:4], in1=lns[:, 0:4], op=OP.mult),
                inc="SV", sd=True)
            E("v", lambda v: v.scalar_tensor_tensor(
                out=lns[:, 4:8], in0=lns[:, 4:8], scalar=1e-5, in1=lns[:, 8:12],
                op0=OP.add, op1=OP.subtract), inc="SV", sd=True)
            E("s", lambda s: s.activation(out=lns[:, 4:8], in_=lns[:, 4:8], func=AF.Ln),
              waits=[("SV", P.val["SV"])], inc="SS", sd=True)
            m_rstd = E("s", lambda s: s.activation(
                out=lns[:, 4:8], in_=lns[:, 4:8], func=AF.Exp, scale=-0.5),
                inc="SS", sd=True)

            for tt in range(NTT):
                E("v", lambda v, tt=tt: v.tensor_scalar(
                    out=lnx[:, 0, :], in0=xbuf[:, tt, :],
                    scalar1=lns[:, tt:tt + 1], scalar2=lns[:, 4 + tt:5 + tt],
                    op0=OP.subtract, op1=OP.mult),
                    waits=[("SS", m_rstd), ("ST", M.get("tp_last", 0))], inc="SV")
                m_ts = P.val["SV"]
                if tt == NTT - 1:
                    M["xfree"] = m_ts
                for ds in range(NKD):
                    par = (tt * NKD + ds) % 2
                    E("t", lambda t, ds=ds, par=par: t.transpose(
                        pT[:, par, 0:128], lnx[:, 0, ds * 128:(ds + 1) * 128], id_s),
                        waits=[("SV", m_ts), ("SV", M.get(("ptev", par), 0))], inc="ST")
                    M["tp_last"] = P.val["ST"]
                    E("v", lambda v, ds=ds, tt=tt, par=par: v.tensor_copy(
                        out=lnxT[:, ds, tt * 128:(tt + 1) * 128], in_=pT[:, par, 0:128]),
                        waits=[("ST", P.val["ST"])], inc="SV")
                    M[("ptev", par)] = P.val["SV"]
            m_lnxT = P.val["SV"]

            if ci == 0:
                E("v", lambda v: v.memset(xtb[:, :, 0:3], 0.0), inc="SV")
            else:
                E("v", lambda v: v.tensor_copy(
                    out=xtb[:, :, 0:3], in_=xtb[:, :, CHT:CHT + 3]), inc="SV")

            for o in range(2 * NDT):
                slot = o % 2
                wld = [("ST", M[("ip_mm", slot)])] if ("ip_mm", slot) in M else []
                mw = E("sy", lambda g_, o=o, slot=slot: g_.dma_start(
                    out=winsl[:, slot],
                    in_=winf[:, o * 128:(o + 1) * 128].rearrange("(kd p) o -> p kd o", p=128)),
                    waits=wld + ([("SG", m_wload)] + wf_tot if g == 0 and o < 2 else []),
                    inc="A%d" % slot, inc_n=16)
                for k in range(NKD):
                    w2 = []
                    if k == 0:
                        w2 = [("A%d" % slot, mw), ("SV", m_lnxT)]
                        if ("ip_ev", slot) in M:
                            w2.append(M[("ip_ev", slot)])
                        if g > 0 and o == 0:
                            w2.append(("SV", M["radd_last"]))
                    E("t", lambda t, k=k, slot=slot: t.matmul(
                        pA[:, slot, :], winsl[:, slot, k, :], lnxT[:, k, :],
                        start=(k == 0), stop=(k == NKD - 1)),
                        waits=w2, inc="ST" if k == NKD - 1 else None)
                M[("ip_mm", slot)] = P.val["ST"]
                if o < NDT:
                    E("v", lambda v, o=o, slot=slot: v.tensor_copy(
                        out=xtb[:, o, 3:CHT + 3], in_=pA[:, slot, :]),
                        waits=[("ST", M[("ip_mm", slot)])], inc="SV")
                    M[("ip_ev", slot)] = ("SV", P.val["SV"])
                else:
                    w3 = [("ST", M[("ip_mm", slot)])]
                    if g > 0 and o == NDT:
                        w3.append(("SV", M["yall_last"]))
                    E("s", lambda s, o=o, slot=slot: s.activation(
                        out=zsil[:, o - NDT, :], in_=pA[:, slot, :], func=AF.Silu),
                        waits=w3, inc="SS")
                    M[("ip_ev", slot)] = ("SS", P.val["SS"])

            for dt in range(NDT):
                w = [("SS", M[("silu", dt - 1)])] if ("silu", dt - 1) in M else \
                    ([("SS", M[("silu", NDT - 1)])] if ("silu", NDT - 1) in M else [])
                E("v", lambda v, dt=dt: v.tensor_scalar(
                    out=cacc, in0=xtb[:, dt, 0:CHT], scalar1=cw[:, dt, 0:1],
                    scalar2=None, op0=OP.mult), waits=w, inc="SV", sd=True)
                for k in range(1, 4):
                    E("v", lambda v, dt=dt, k=k: v.scalar_tensor_tensor(
                        out=cacc, in0=xtb[:, dt, k:CHT + k], scalar=cw[:, dt, k:k + 1],
                        in1=cacc, op0=OP.mult, op1=OP.add), inc="SV", sd=True)
                w = [("SV", P.val["SV"])]
                if g > 0 and dt == 0:
                    w.append(("SV", M["yall_last"]))
                E("s", lambda s, dt=dt: s.activation(
                    out=xct[:, dt, :], in_=cacc, func=AF.Silu, bias=cb[:, dt:dt + 1]),
                    waits=w, inc="SS")
                M[("silu", dt)] = P.val["SS"]
            m_silu_all = P.val["SS"]

            for tt in range(NTT):
                par = tt % 2
                for kd in range(NDT):
                    w = []
                    if kd == 0:
                        w = [("SS", m_silu_all)]
                        if ("xp_ev", par) in M:
                            w.append(("SV", M[("xp_ev", par)]))
                        elif g > 0:
                            w.append(("SV", M["radd_last"]))
                    E("t", lambda t, tt=tt, kd=kd, par=par: t.matmul(
                        pB[:, par, 0:96], xct[:, kd, tt * 128:(tt + 1) * 128], wxp[:, kd, :],
                        start=(kd == 0), stop=(kd == NDT - 1)),
                        waits=w, inc="ST" if kd == NDT - 1 else None)
                E("v", lambda v, tt=tt, par=par: v.tensor_copy(
                    out=xdbl[:, tt, :], in_=pB[:, par, 0:96]),
                    waits=[("ST", P.val["ST"])], inc="SV")
                M[("xp_ev", par)] = P.val["SV"]
            m_xdbl = P.val["SV"]

            for tt in range(NTT):
                E("t", lambda t, tt=tt: t.transpose(
                    pS[:, tt * 128:(tt + 1) * 128], xdbl[:, tt, 0:RNK], id_s),
                    waits=[("SV", m_xdbl), ("SV", M.get(("ptev", 0), 0))] if tt == 0 else [],
                    inc="ST" if tt == NTT - 1 else None)
            E("v", lambda v: v.tensor_copy(out=dtlT, in_=pS),
              waits=[("ST", P.val["ST"])], inc="SV")
            M[("ptev", 0)] = P.val["SV"]
            for tt in range(NTT):
                E("t", lambda t, tt=tt: t.transpose(
                    pBC[:, 1, tt * 128:(tt + 1) * 128], xdbl[:, tt, RNK:RNK + NST], id_s),
                    waits=[("SV", M.get(("ptev", 1), 0))] if tt == 0 else [],
                    inc="ST" if tt == NTT - 1 else None)
            E("v", lambda v: v.tensor_copy(out=btile, in_=pBC[:, 1, :]),
              waits=[("ST", P.val["ST"])], inc="SV")
            M[("ptev", 1)] = P.val["SV"]
            for tt in range(NTT):
                E("t", lambda t, tt=tt: t.transpose(
                    pBC[:, 0, tt * 128:(tt + 1) * 128], xdbl[:, tt, RNK + NST:96], id_s),
                    waits=[("SV", M.get(("ptev", 0), 0))] if tt == 0 else [],
                    inc="ST" if tt == NTT - 1 else None)
            E("v", lambda v: v.tensor_copy(out=ctile, in_=pBC[:, 0, :]),
              waits=[("ST", P.val["ST"])], inc="SV")
            M[("ptev", 0)] = P.val["SV"]
            m_bc = P.val["SV"]

            for n in range(NST):
                par = n % 2
                w = [("SV", m_bc)] if n == 0 else []
                if n <= 1:
                    w.append(M[("ip_ev", par)])
                if ("rp_ev", par, 0) in M and n >= 2:
                    w.append(("SV", M[("rp_ev", par, 0)]))
                E("t", lambda t, n=n, par=par: t.matmul(
                    pA[:, par, :], sels_s[:, n * 128:(n + 1) * 128], btile,
                    start=True, stop=True), waits=w, inc="ST")
                mb = P.val["ST"]
                w = []
                if n >= 2 and ("rp_ev", par, 1) in M:
                    w.append(("SV", M[("rp_ev", par, 1)]))
                E("t", lambda t, n=n, par=par: t.matmul(
                    pB[:, par, :], sels_s[:, n * 128:(n + 1) * 128], ctile,
                    start=True, stop=True), waits=w, inc="ST")
                mc = P.val["ST"]
                E("v", lambda v, n=n, par=par: v.tensor_copy(
                    out=brep[:, n, :], in_=pA[:, par, :]), waits=[("ST", mb)], inc="SV")
                M[("rp_ev", par, 0)] = P.val["SV"]
                E("v", lambda v, n=n, par=par: v.tensor_copy(
                    out=crep[:, n, :], in_=pB[:, par, :]), waits=[("ST", mc)], inc="SV")
                M[("rp_ev", par, 1)] = P.val["SV"]

            for dt in range(NDT):
                par = dt % 2
                w = [("SV", M[("rp_ev", par, 0)])] if dt <= 1 else [("SV", M[("dt_ev", par)])]
                E("t", lambda t, dt=dt, par=par: t.matmul(
                    pA[:, par, :], wdt[:, dt * 128:(dt + 1) * 128], dtlT,
                    start=True, stop=True), waits=w, inc="ST")
                wv = [("ST", P.val["ST"])]
                if dt:
                    wv.append(("SS", P.val["SS"]))
                elif g > 0:
                    wv.append(("SV", M["ut_last"]))
                E("v", lambda v, dt=dt, par=par: v.tensor_scalar(
                    out=spt, in0=pA[:, par, :], scalar1=dtb_s[:, dt:dt + 1], scalar2=30.0,
                    op0=OP.add, op1=OP.min), waits=wv, inc="SV", sd=True)
                M[("dt_ev", par)] = P.val["SV"]
                E("s", lambda s: s.activation(out=spe, in_=spt, func=AF.Exp),
                  waits=[("SV", P.val["SV"])], inc="SS", sd=True)
                wl = []
                if g > 0 and dt == 0:
                    wl.append(("SV", M["ut_last"]))
                E("s", lambda s, dt=dt: s.activation(
                    out=delta[:, dt, :], in_=spe, func=AF.Ln, bias=1.0),
                    waits=wl, inc="SS", sd=True)
            m_delta = P.val["SS"]

            if ci == 0:
                E("v", lambda v, g=g: v.memset(hc[:, g % 2, :], 0.0), inc="SV")
            for dt in range(NDT):
                E("v", lambda v, dt=dt: v.tensor_tensor(
                    out=ut, in0=delta[:, dt, :], in1=xct[:, dt, :], op=OP.mult),
                    waits=[("SS", P.val["SS"] if dt else m_delta)], inc="SV", sd=True)
                if dt == NDT - 1:
                    M["ut_last"] = P.val["SV"]
                for n in range(NST):
                    sl = n % 2
                    key = g * 256 + dt * NST + n
                    w = []
                    if key >= 2 and (key - 2) in m_tts:
                        w = [("SV", m_tts[key - 2])]
                    E("s", lambda s, dt=dt, n=n, sl=sl: s.activation(
                        out=aslt[:, sl, :], in_=delta[:, dt, :], func=AF.Exp,
                        scale=nega[:, dt, n:n + 1]), waits=w, inc="SS")
                    m_a = P.val["SS"]
                    E("v", lambda v, n=n: v.tensor_tensor(
                        out=ubs, in0=ut, in1=brep[:, n, :], op=OP.mult), inc="SV", sd=True)
                    m_tts[key] = E("v", lambda v, n=n, sl=sl, g=g, dt=dt: v.tensor_tensor_scan(
                        out=hbig[:, n, :], data0=aslt[:, sl, :], data1=ubs,
                        initial=hc[:, g % 2, dt * NST + n:dt * NST + n + 1],
                        op0=OP.mult, op1=OP.add),
                        waits=[("SS", m_a)], inc="SV", sd=True)
                E("v", lambda v, dt=dt, g=g: v.tensor_copy(
                    out=hc[:, (g + 1) % 2, dt * NST:(dt + 1) * NST],
                    in_=hbig[:, :, CHT - 1]), inc="SV", sd=True)
                E("v", lambda v: v.tensor_tensor(
                    out=hbig, in0=hbig, in1=crep, op=OP.mult), inc="SV", sd=True)
                E("v", lambda v: v.tensor_tensor(
                    out=hbig[:, 0:8, :], in0=hbig[:, 0:8, :], in1=hbig[:, 8:16, :],
                    op=OP.add), inc="SV", sd=True)
                E("v", lambda v: v.tensor_tensor(
                    out=hbig[:, 0:4, :], in0=hbig[:, 0:4, :], in1=hbig[:, 4:8, :],
                    op=OP.add), inc="SV", sd=True)
                E("v", lambda v: v.tensor_tensor(
                    out=hbig[:, 0:2, :], in0=hbig[:, 0:2, :], in1=hbig[:, 2:4, :],
                    op=OP.add), inc="SV", sd=True)
                E("v", lambda v: v.tensor_tensor(
                    out=ytmp, in0=hbig[:, 0, :], in1=hbig[:, 1, :], op=OP.add),
                    inc="SV", sd=True)
                E("v", lambda v, dt=dt: v.scalar_tensor_tensor(
                    out=ytmp, in0=xct[:, dt, :], scalar=dp_s[:, dt:dt + 1], in1=ytmp,
                    op0=OP.mult, op1=OP.add), inc="SV", sd=True)
                w = [("ST", M.get(("wo_k", 1, dt), 0))]
                if dt == 0:
                    w.append(M[("ip_ev", 1)])
                E("v", lambda v, dt=dt: v.tensor_tensor(
                    out=yall[:, dt, :], in0=ytmp, in1=zsil[:, dt, :], op=OP.mult),
                    waits=w, inc="SV", sd=True)
                M[("gate", dt)] = P.val["SV"]
            M["yall_last"] = P.val["SV"]

            if debug and g == 0:
                E("sy", lambda sy: sy.dma_start(out=dbg_wf, in_=woutf),
                  waits=W("g"), inc="SR", inc_n=16)
                for nm, tns in (("dbg_xct", xct), ("dbg_delta", delta), ("dbg_xdbl", xdbl),
                                ("dbg_brep", brep), ("dbg_y", yall), ("dbg_lnxT", lnxT)):
                    E("sy", lambda sy, nm=nm, tns=tns: sy.dma_start(
                        out={"dbg_xct": dbg_xct, "dbg_delta": dbg_delta,
                             "dbg_xdbl": dbg_xdbl, "dbg_brep": dbg_brep,
                             "dbg_y": dbg_y, "dbg_lnxT": dbg_lnxT}[nm], in_=tns),
                        waits=W("v", "s"), inc="SR", inc_n=16)

            for fh in range(2):
                for k in range(NDT):
                    slot = k % 2
                    wld = [("ST", M[("wo_mm", slot)])] if ("wo_mm", slot) in M else []
                    mwo = E("sy", lambda g_, k=k, fh=fh, slot=slot: g_.dma_start(
                        out=woutsl[:, slot, :],
                        in_=woutf[k * 128:(k + 1) * 128, fh * 512:(fh + 1) * 512]),
                        waits=wld + (wf_tot if g == 0 and fh == 0 and k < 2 else []),
                        inc="O%d" % slot, inc_n=16)
                    for tt in range(NTT):
                        pp = pA if tt < 2 else pB
                        w2 = [("O%d" % slot, mwo)] if tt == 0 else []
                        if tt == 0 and fh == 0:
                            w2.append(("SV", M[("gate", k)]))
                        if k == 0:
                            if fh == 0:
                                w2.append(("SV", M[("dt_ev", tt % 2)] if tt < 2
                                           else M[("rp_ev", tt % 2, 1)]))
                            else:
                                w2.append(("SV", M[("radd", fh - 1, tt)]))
                        E("t", lambda t, k=k, tt=tt, slot=slot, pp=pp: t.matmul(
                            pp[:, tt % 2, :], yall[:, k, tt * 128:(tt + 1) * 128],
                            woutsl[:, slot, :],
                            start=(k == 0), stop=(k == NDT - 1)),
                            waits=w2, inc="ST" if tt == NTT - 1 else None)
                    M[("wo_mm", slot)] = P.val["ST"]
                    M[("wo_k", fh, k)] = P.val["ST"]
                m_opmm = P.val["ST"]
                for tt in range(NTT):
                    pp = pA if tt < 2 else pB
                    sl = (fh * NTT + tt) % 3
                    i = fh * NTT + tt
                    wxr = [("SV", M[("radd", (i - 3) // NTT, (i - 3) % NTT)])] if i >= 3 \
                        else ([("SV", M.get(("radd", 1, i + 1), 0))] if g > 0 else [])
                    mxr = E("sy", lambda sy, b=b, t0=t0, tt=tt, fh=fh, sl=sl: sy.dma_start(
                        out=xr[:, sl, :],
                        in_=xin[b, t0 + tt * 128:t0 + (tt + 1) * 128,
                                fh * 512:(fh + 1) * 512]),
                        waits=wxr, inc="XR%d" % sl, inc_n=16)
                    E("v", lambda v, tt=tt, fh=fh, pp=pp, sl=sl, mxr=mxr: v.tensor_tensor(
                        out=rsl[:, tt % 2, :], in0=pp[:, tt % 2, :], in1=xr[:, sl, :],
                        op=OP.add),
                        waits=[("ST", m_opmm), ("XR%d" % sl, mxr), ("SR", P.val["SR"])],
                        inc="SV")
                    M[("radd", fh, tt)] = P.val["SV"]
                    E("sy", lambda sy, b=b, t0=t0, tt=tt, fh=fh: sy.dma_start(
                        out=resid[b, t0 + tt * 128:t0 + (tt + 1) * 128,
                                  fh * 512:(fh + 1) * 512], in_=rsl[:, tt % 2, :]),
                        waits=[("SV", P.val["SV"])], inc="SR", inc_n=16)
            M["radd_last"] = P.val["SV"]

        m_resid_done = P.val["SR"]
        m_scan_end = M["radd_last"]
        residR = resid.rearrange("b l d -> (b l) d")
        xbufV = xbuf.rearrange("p a b -> p (a b)")
        zsilV = zsil.rearrange("p a b -> p (a b)")
        xctV = xct.rearrange("p a b -> p (a b)")
        deltaV = delta.rearrange("p a b -> p (a b)")
        brepV = brep.rearrange("p a b -> p (a b)")
        crepV = crep.rearrange("p a b -> p (a b)")
        lnxTV = lnxT.rearrange("p a b -> p (a b)")
        w2p = [ubs, ut, cacc, ytmp]

        def lxg_ap(ctl, lo, hi):
            base = (ctl % 8) * DM
            return (xctV if ctl < 8 else deltaV)[:, base + lo:base + hi]

        def xgT_ap(k, c0, c1):
            base = (k % 4) * 2048
            return (brepV if k < 4 else crepV)[:, base + c0:base + c1]

        def hq_ap(f, t0, t1):
            return (hbig if f < 16 else yall)[:, f % 16, t0:t1]

        Mf = {}
        E("sy", lambda sy: sy.dma_start(out=ghx[:, 0:GW], in_=gidx),
          inc="SX", inc_n=16)
        m_ghx = P.val["SX"]

        def ffn_params(e):
            E("sy", lambda sy, e=e: sy.dma_start(out=b1s[:, e % 2, :], in_=b1e[e]),
              waits=[("SV", Mf["gelu_ts_last"])] if e >= 2 else [],
              inc="B%d" % (e % 2), inc_n=16)
            bw = [("SV", Mf["w2evac_last"])] if e >= 2 else [("SV", M["xfree"])]
            E("sy", lambda sy, e=e: sy.dma_start(
                out=xbuf[:, 2 + e % 2, :], in_=b2e[e]), waits=bw,
                inc="B%d" % (e % 2), inc_n=16)
            Mf[("bb", e % 2)] = P.val["B%d" % (e % 2)]

        def ffn_gather_ln(e):
            wt = [("TK", P.val["TK"])]
            if "g_last" not in Mf:
                wt.append(("SR", m_resid_done))
            m_tok = E("gp", lambda g, e=e: g.dma_start(out=tok_s, in_=toka[e]),
                      waits=wt, inc="TK", inc_n=16)
            for ctl in range(NCT):
                sl = ctl % 2
                gw = [("TK", m_tok)]
                if e == 0 and ctl < 2:
                    gw += [("SR", m_resid_done), ("SV", M["xfree"])]
                if ("lnr", (ctl - 2) % NCT) in Mf and (e > 0 or ctl >= 2):
                    gw.append(("SV", Mf[("lnr", (ctl - 2) % NCT)]))
                m_g = E("gp", lambda g, ctl=ctl, sl=sl: g.indirect_dma_start(
                    out=xbufV[:, sl * DM:(sl + 1) * DM], out_offset=None,
                    in_=residR[0:128, :],
                    in_offset=bass.IndirectOffsetOnAxis(ap=tok_s[:, ctl:ctl + 1], axis=0)),
                    waits=gw, inc="GA%d" % sl, inc_n=16)
                cww = []
                if ctl == 0 and e >= 2:
                    cww = [("SV", Mf["w2evac_last"])]
                E("gp", lambda g, e=e, ctl=ctl: g.indirect_dma_start(
                    out=wcmb[:, e % 2, ctl, :], out_offset=None, in_=cwa[0:128, :],
                    in_offset=bass.IndirectOffsetOnAxis(ap=tok_s[:, ctl:ctl + 1], axis=0)),
                    waits=cww, inc="GB%d" % (e % 2), inc_n=16)
                Mf["g_last"] = True
                Mf[("wcmb", e % 2)] = P.val["GB%d" % (e % 2)]
                xg = xbufV[:, sl * DM:(sl + 1) * DM]
                E("v", lambda v, xg=xg: v.tensor_reduce(
                    out=lns[:, 0:1], in_=xg, axis=AX.X, op=OP.add),
                    waits=[("GA%d" % sl, m_g)], inc="SV")
                sqw = [("GA%d" % sl, m_g), ("SV", P.val["SV"]), ("SS", P.val["SS"])]
                if e == 0 and ctl == 0:
                    sqw.append(("SR", m_resid_done))
                E("s", lambda s, xg=xg: s.activation(out=sqt[:, 0, :], in_=xg,
                                                     func=AF.Square),
                  waits=sqw, inc="SS")
                E("v", lambda v: v.tensor_reduce(
                    out=lns[:, 1:2], in_=sqt[:, 0, :], axis=AX.X, op=OP.add),
                    waits=[("SS", P.val["SS"])], inc="SV")
                E("v", lambda v: v.tensor_scalar(
                    out=lns[:, 0:2], in0=lns[:, 0:2], scalar1=1.0 / DM, scalar2=None,
                    op0=OP.mult), inc="SV", sd=True)
                E("v", lambda v: v.tensor_tensor(
                    out=lns[:, 2:3], in0=lns[:, 0:1], in1=lns[:, 0:1], op=OP.mult),
                    inc="SV", sd=True)
                E("v", lambda v: v.scalar_tensor_tensor(
                    out=lns[:, 1:2], in0=lns[:, 1:2], scalar=1e-5, in1=lns[:, 2:3],
                    op0=OP.add, op1=OP.subtract), inc="SV", sd=True)
                E("s", lambda s: s.activation(out=lns[:, 1:2], in_=lns[:, 1:2], func=AF.Ln),
                  waits=[("SV", P.val["SV"])], inc="SS", sd=True)
                E("s", lambda s: s.activation(out=lns[:, 1:2], in_=lns[:, 1:2],
                                              func=AF.Exp, scale=-0.5), inc="SS", sd=True)
                lxw = [("SS", P.val["SS"])]
                if ctl == 0 and "tp_last" in Mf:
                    lxw.append(("ST", Mf["tp_last"]))
                E("v", lambda v, ctl=ctl, xg=xg: v.tensor_scalar(
                    out=lxg_ap(ctl, 0, DM), in0=xg, scalar1=lns[:, 0:1],
                    scalar2=lns[:, 1:2], op0=OP.subtract, op1=OP.mult),
                    waits=lxw, inc="SV")
                Mf[("lnr", ctl)] = P.val["SV"]
            Mf["lxg"] = P.val["SV"]

        def ffn_transpose(e):
            m_lxg = Mf["lxg"]
            for ds in range(NKD):
                for cq in range(4):
                    par = (ds * 4 + cq) % 2
                    for j in range(4):
                        ctl = cq * 4 + j
                        wts = []
                        if ds == 0 and cq == 0 and j == 0:
                            wts = [("SV", m_lxg), ("ST", Mf.get("w1_last", 0))]
                        if j == 0:
                            wts.append(("SV", M.get(("ptev", par), 0)))
                        E("t", lambda t, ds=ds, ctl=ctl, j=j, par=par: t.transpose(
                            pT[:, par, j * 128:(j + 1) * 128],
                            lxg_ap(ctl, ds * 128, (ds + 1) * 128), id_s),
                            waits=wts, inc="ST" if j == 3 else None)
                    E("v", lambda v, ds=ds, cq=cq, par=par: v.tensor_copy(
                        out=xgT_ap(ds, cq * 512, (cq + 1) * 512), in_=pT[:, par, :]),
                        waits=[("ST", P.val["ST"])], inc="SV")
                    M[("ptev", par)] = P.val["SV"]
            Mf["xgT"] = P.val["SV"]
            Mf["tp_last"] = P.val["ST"]

        def ffn_w1w2(e):
            m_xgT = Mf["xgT"]

            def gelu_b(f):
                ps = f % 2
                gx = zsilV[:, ps * 512:(ps + 1) * 512]
                gsq = zsilV[:, (2 + ps) * 512:(3 + ps) * 512]
                gx3 = zsilV[:, (4 + ps) * 512:(5 + ps) * 512]
                gth = zsilV[:, (6 + ps) * 512:(7 + ps) * 512]
                E("v", lambda v, gx=gx, gsq=gsq, gx3=gx3: v.tensor_tensor(
                    out=gx3, in0=gsq, in1=gx, op=OP.mult),
                    waits=[("SS", Mf[("g_sq", ps)])], inc="SV")
                E("v", lambda v, gx=gx, gx3=gx3: v.scalar_tensor_tensor(
                    out=gx3, in0=gx3, scalar=0.044715, in1=gx,
                    op0=OP.mult, op1=OP.add), inc="SV", sd=True)
                E("s", lambda s, gx3=gx3, gth=gth: s.activation(
                    out=gth, in_=gx3, func=AF.Sigmoid, scale=2.0 * GELU_C),
                    waits=[("SV", P.val["SV"])], inc="SS")
                Mf[("g_sg", ps)] = P.val["SS"]
                hw = [("SS", P.val["SS"])]
                if "w2_mm_last" in Mf:
                    hw.append(("ST", Mf["w2_mm_last"]))
                E("v", lambda v, f=f, gx=gx, gth=gth: v.tensor_tensor(
                    out=hq_ap(f, 0, 512), in0=gth, in1=gx, op=OP.mult),
                    waits=hw, inc="SV", sd=True)
                Mf[("g6", f)] = P.val["SV"]

            for q in range(4):
                for f in range(NFT):
                    ws = f % 4
                    ps = f % 2
                    lw = []
                    if q == 0 and f < 4:
                        lw.append(("CA", m_cast[e]))
                    if ("w1_mm", ws) in Mf:
                        lw.append(("ST", Mf[("w1_mm", ws)]))
                    elif e == 0 and ws < 2:
                        lw.append(("ST", M[("ip_mm", ws)]))
                    mwl = E("sy", lambda sy, e=e, f=f, ws=ws: sy.dma_start(
                        out=winsl[:, ws], in_=w1H[e, f]),
                        waits=lw, inc="U%d" % ws, inc_n=16)
                    for k in range(NKD):
                        w_ = []
                        if k == 0:
                            w_ = [("U%d" % ws, mwl)]
                            if ("gelu_evac", ps) in Mf:
                                w_.append(("SV", Mf[("gelu_evac", ps)]))
                            if f < 2 and q == 0:
                                w_.append(("SV", m_xgT))
                        E("t", lambda t, q=q, f=f, k=k, ws=ws, ps=ps: t.matmul(
                            pC[:, ps, :], winsl[:, ws, k, :],
                            xgT_ap(k, q * 512, (q + 1) * 512),
                            start=(k == 0), stop=(k == NKD - 1)),
                            waits=w_, inc="ST" if k == NKD - 1 else None)
                    Mf[("w1_mm", ws)] = P.val["ST"]
                    gx = zsilV[:, ps * 512:(ps + 1) * 512]
                    gsq = zsilV[:, (2 + ps) * 512:(3 + ps) * 512]
                    gx3 = zsilV[:, (4 + ps) * 512:(5 + ps) * 512]
                    gth = zsilV[:, (6 + ps) * 512:(7 + ps) * 512]
                    gw1 = [("ST", Mf[("w1_mm", ws)])]
                    if q == 0 and f == 0:
                        gw1.append(("B%d" % (e % 2), Mf[("bb", e % 2)]))
                    if ("g_sg", ps) in Mf:
                        gw1.append(("SS", Mf[("g_sg", ps)]))
                    E("v", lambda v, e=e, f=f, ps=ps, gx=gx: v.tensor_scalar(
                        out=gx, in0=pC[:, ps, :], scalar1=b1s[:, e % 2, f:f + 1],
                        scalar2=None, op0=OP.add), waits=gw1, inc="SV")
                    Mf[("gelu_evac", ps)] = P.val["SV"]
                    Mf["gelu_ts_last"] = P.val["SV"]
                    E("s", lambda s, gx=gx, gsq=gsq: s.activation(
                        out=gsq, in_=gx, func=AF.Square),
                        waits=[("SV", P.val["SV"])], inc="SS")
                    Mf[("g_sq", ps)] = P.val["SS"]
                    if f > 0:
                        gelu_b(f - 1)
                gelu_b(NFT - 1)
                if q == 2 and e + 1 < NEXP:
                    ffn_params(e + 1)
                    ffn_gather_ln(e + 1)
                if q == 3:
                    Mf["w1_last"] = P.val["ST"]
                    if e + 1 < NEXP:
                        ffn_transpose(e + 1)
                for dh in range(2):
                    for k in range(NFT):
                        ks = k % 4
                        piece = w2p[ks]
                        pw = []
                        if q == 0 and dh == 0 and k < 4:
                            pw.append(("CA", m_cast[e]))
                        if ("w2_ld", ks) in Mf:
                            pw.append(("ST", Mf[("w2_ld", ks)]))
                        elif e == 0:
                            pw.append(("SV", m_scan_end))
                        wsem = ("W0", "W1", "W2b", "W3")[ks]
                        mw2 = E("sy", lambda sy, e=e, dh=dh, k=k, piece=piece: sy.dma_start(
                            out=piece, in_=w2H[e, dh, k * 128:(k + 1) * 128, :]),
                            waits=pw, inc=wsem, inc_n=16)
                        for tt in range(NTT):
                            pp = pA if tt < 2 else pB
                            w_ = [(wsem, mw2)] if tt == 0 else []
                            if tt == 0 and dh == 0:
                                w_.append(("SV", Mf[("g6", k)]))
                            if k == 0:
                                if dh == 0:
                                    if "w2evac_last" in Mf:
                                        w_.append(("SV", Mf["w2evac_last"]))
                                    else:
                                        w_.append(("SV", m_scan_end))
                                else:
                                    w_.append(("SV", Mf["w2evac_dh0"]))
                            E("t", lambda t, k=k, tt=tt, piece=piece, pp=pp: t.matmul(
                                pp[:, tt % 2, :], hq_ap(k, tt * 128, (tt + 1) * 128),
                                piece, start=(k == 0), stop=(k == NFT - 1)),
                                waits=w_, inc="ST" if tt == NTT - 1 else None)
                        Mf[("w2_ld", ks)] = P.val["ST"]
                    Mf["w2_mm_last"] = P.val["ST"]
                    m_pass = P.val["ST"]
                    for tt in range(NTT):
                        ct = q * 4 + tt
                        pp = pA if tt < 2 else pB
                        wv = [("ST", m_pass)]
                        if dh == 0 and tt == 0:
                            wv.append(("B%d" % (e % 2), Mf[("bb", e % 2)]))
                            wv.append(("GB%d" % (e % 2), Mf[("wcmb", e % 2)]))
                        if ("wgt_st", tt) in Mf:
                            wv.append(("R%d" % tt, Mf[("wgt_st", tt)]))
                        dst = lnxTV[:, tt * 1024 + dh * 512:tt * 1024 + dh * 512 + 512]
                        b2rr = xbufV[:, (2 + e % 2) * DM + dh * 512:
                                     (2 + e % 2) * DM + dh * 512 + 512]
                        E("v", lambda v, pp=pp, tt=tt, dst=dst, b2rr=b2rr: v.tensor_tensor(
                            out=dst, in0=pp[:, tt % 2, :], in1=b2rr, op=OP.add),
                            waits=wv, inc="SV")
                        E("v", lambda v, e=e, ct=ct, dst=dst: v.tensor_scalar(
                            out=dst, in0=dst, scalar1=wcmb[:, e % 2, ct, e:e + 1],
                            scalar2=None, op0=OP.mult), inc="SV", sd=True)
                    if dh == 0:
                        Mf["w2evac_dh0"] = P.val["SV"]
                    else:
                        Mf["w2evac_last"] = P.val["SV"]
                        for tt in range(NTT):
                            ct = q * 4 + tt
                            E("sy", lambda sy, e=e, ct=ct, tt=tt: sy.dma_start(
                                out=wgtH[e * NCFF + ct * 128:e * NCFF + (ct + 1) * 128, :],
                                in_=lnxTV[:, tt * 1024:(tt + 1) * 1024]),
                                waits=[("SV", P.val["SV"])], inc="R%d" % tt, inc_n=16)
                            Mf[("wgt_st", tt)] = P.val["R%d" % tt]

        ffn_params(0)
        ffn_gather_ln(0)
        ffn_transpose(0)
        for e in range(NEXP):
            ffn_w1w2(e)

        m_wgt_all = P.val["SR"]
        wgt_tot = [("R%d" % t, P.val["R%d" % t]) for t in range(4)]
        if debug:
            E("sy", lambda sy: sy.dma_start(out=dbgW, in_=wgtH[0:NEXP * NCFF, :]),
              waits=[("SR", m_wgt_all)] + wgt_tot, inc="SR", inc_n=16)
            m_wgt_all = P.val["SR"]
        E("v", lambda v: v.memset(lnx[:, 0, :], 0.0), inc="SV", sd=True)
        E("sy", lambda sy: sy.dma_start(
            out=wgtH[NEXP * NCFF:NEXP * NCFF + 1, :], in_=lnx[0:1, 0, :]),
            waits=[("SV", P.val["SV"]), ("SR", m_resid_done)], inc="SR", inc_n=16)
        m_zrow = P.val["SR"]
        for rt in range(NROW // 128):
            sl = rt % 2
            rw = [("SV", Mf[("cadd", rt - 2)])] if ("cadd", rt - 2) in Mf \
                else [("SV", P.val["SV"]), ("SS", P.val["SS"])]
            acc = xbufV[:, sl * DM:(sl + 1) * DM]
            m_r = E("sy", lambda sy, rt=rt, acc=acc: sy.dma_start(
                out=acc, in_=residR[rt * 128:(rt + 1) * 128, :]),
                waits=rw, inc="XA%d" % sl, inc_n=16)
            for j in range(3):
                gws = []
                if j == 0:
                    gws = [("SR", m_zrow), ("SX", m_ghx)]
                    if rt == 0:
                        gws += [("SR", m_wgt_all), ("ST", Mf["w2_mm_last"]),
                                ("SS", P.val["SS"]), ("SV", P.val["SV"])] + wgt_tot
                    if ("cadd", rt - 2) in Mf:
                        gws.append(("SV", Mf[("cadd", rt - 2)]))
                E("gp", lambda g, rt=rt, sl=sl, j=j: g.indirect_dma_start(
                    out=zsilV[:, (sl * 3 + j) * DM:(sl * 3 + j + 1) * DM],
                    out_offset=None, in_=wgtH[0:128, :],
                    in_offset=bass.IndirectOffsetOnAxis(
                        ap=ghx[:, j * 64 + rt:j * 64 + rt + 1], axis=0)),
                    waits=gws, inc="GC%d" % sl, inc_n=16)
            m_gg = P.val["GC%d" % sl]
            for j in range(3):
                aw = [("XA%d" % sl, m_r), ("GC%d" % sl, m_gg)] if j == 0 else []
                E("v", lambda v, sl=sl, j=j, acc=acc: v.tensor_tensor(
                    out=acc, in0=acc,
                    in1=zsilV[:, (sl * 3 + j) * DM:(sl * 3 + j + 1) * DM], op=OP.add),
                    waits=aw, inc="SV", sd=True)
            Mf[("cadd", rt)] = P.val["SV"]
            E("sy", lambda sy, rt=rt, acc=acc: sy.dma_start(
                out=final01[rt * 128:(rt + 1) * 128, :], in_=acc),
                waits=[("SV", P.val["SV"]), ("SR", m_zrow)], inc="SR", inc_n=16)
        m_light = P.val["SR"]
        hacc = xbufV[:, 3 * DM:4 * DM]
        m_hsc = 0
        for ht in range(nht):
            base = 192 + ht * (K2 + 1)
            hw0 = [("SR", m_hsc)] if m_hsc else [("SV", P.val["SV"])]
            m_hr = E("gp", lambda g, base=base, hacc=hacc: g.indirect_dma_start(
                out=hacc, out_offset=None, in_=residR[0:128, :],
                in_offset=bass.IndirectOffsetOnAxis(ap=ghx[:, base:base + 1], axis=0)),
                waits=hw0, inc="HR", inc_n=16)
            for j in range(K2):
                sl2 = 6 + (j % 2)
                if ("hadd", j - 2) in Mf:
                    gw2 = [("SV", Mf[("hadd", j - 2)])]
                else:
                    gw2 = [("SV", P.val["SV"])]
                m_hg = E("gp", lambda g, base=base, j=j, sl2=sl2: g.indirect_dma_start(
                    out=zsilV[:, sl2 * DM:(sl2 + 1) * DM], out_offset=None,
                    in_=wgtH[0:128, :],
                    in_offset=bass.IndirectOffsetOnAxis(
                        ap=ghx[:, base + 1 + j:base + 2 + j], axis=0)),
                    waits=gw2, inc="HG%d" % (j % 2), inc_n=16)
                aw = [("HG%d" % (j % 2), m_hg)] + ([("HR", m_hr)] if j == 0 else [])
                E("v", lambda v, sl2=sl2, hacc=hacc: v.tensor_tensor(
                    out=hacc, in0=hacc,
                    in1=zsilV[:, sl2 * DM:(sl2 + 1) * DM], op=OP.add),
                    waits=aw, inc="SV", sd=True)
                Mf[("hadd", j)] = P.val["SV"]
            for j in range(K2):
                Mf.pop(("hadd", j), None)
            E("gp", lambda g, base=base, hacc=hacc: g.indirect_dma_start(
                out=final01, out_offset=bass.IndirectOffsetOnAxis(
                    ap=ghx[:, base:base + 1], axis=0),
                in_=hacc, in_offset=None),
                waits=[("SV", P.val["SV"]), ("SR", m_light)], inc="SR", inc_n=16)
            m_hsc = P.val["SR"]

        E("sy", lambda sy: None, waits=W("r"))

        with nc.Block() as block:
            P.install(block)
    return nc


def kernel(**inputs):
    hs = np.ascontiguousarray(inputs["L0_attn_hidden_states"], dtype=np.float32)
    def tile_d(a):
        a = np.asarray(a, dtype=np.float32)
        return np.ascontiguousarray(a.reshape(NDT, 128, *a.shape[1:]).swapaxes(0, 1))
    win_t = np.ascontiguousarray(inputs["L0_attn_in_proj_w"].T, dtype=np.float32)
    convw = tile_d(inputs["L0_attn_conv_w"][:, 0, :])
    convb = tile_d(inputs["L0_attn_conv_b"])
    xpw_t = tile_d(np.asarray(inputs["L0_attn_x_proj_w"], dtype=np.float32).T)
    wdt_t = np.ascontiguousarray(inputs["L0_attn_dt_proj_w"].T, dtype=np.float32)
    dtbv = tile_d(inputs["L0_attn_dt_proj_b"])
    alog = tile_d(inputs["L0_attn_A_log"])
    dpar = tile_d(inputs["L0_attn_D_param"])
    wout_t = np.ascontiguousarray(inputs["L0_attn_out_proj_w"].T, dtype=np.float32)
    cmbw = np.ascontiguousarray(inputs["L0_ffn_combine_weights"], dtype=np.float32)
    topi = np.ascontiguousarray(inputs["L0_ffn_top_idx"], dtype=np.int64)
    w1a = np.ascontiguousarray(inputs["L0_ffn_W1"], dtype=np.float32)
    b1a = np.ascontiguousarray(inputs["L0_ffn_b1"], dtype=np.float32)
    w2a = np.ascontiguousarray(inputs["L0_ffn_W2"], dtype=np.float32)
    b2a = np.ascontiguousarray(inputs["L0_ffn_b2"], dtype=np.float32)

    sels = np.zeros((NST, NST * 128), dtype=np.float16)
    for n in range(NST):
        sels[n, n * 128:(n + 1) * 128] = 1.0
    ident = np.eye(128, dtype=np.float16)

    DUMMY = NEXP * NCFF
    tgt = np.ascontiguousarray(topi.T).reshape(-1)
    cnt = np.bincount(tgt, minlength=NROW)
    ordf = np.argsort(tgt, kind="stable")
    starts = np.zeros(NROW + 1, np.int64)
    starts[1:] = np.cumsum(cnt)
    L = 3
    g3 = np.full((NROW, L), DUMMY, np.int64)
    light = cnt <= L
    for j in range(L):
        m = light & (cnt > j)
        g3[m, j] = ordf[starts[:-1][m] + j]
    heavy = np.where(cnt > L)[0]
    nh = len(heavy)
    if nh:
        K2 = int(cnt[heavy].max())
        nht = (nh + 127) // 128
        npad = nht * 128 - nh
        pads = np.where(light)[0][:npad]
        hrows = np.concatenate([heavy, pads])
        H = np.full((nht * 128, K2), DUMMY, np.int64)
        for i, r in enumerate(hrows):
            H[i, :cnt[r]] = ordf[starts[r]:starts[r] + cnt[r]]
    else:
        K2, nht = 0, 0
    GW = 192 + nht * (K2 + 1)
    gidx = np.zeros((128, GW), np.int32)
    for rt in range(NROW // 128):
        for j in range(L):
            gidx[:, j * 64 + rt] = g3[rt * 128:(rt + 1) * 128, j]
    for ht in range(nht):
        base = 192 + ht * (K2 + 1)
        gidx[:, base] = hrows[ht * 128:(ht + 1) * 128]
        for j in range(K2):
            gidx[:, base + 1 + j] = H[ht * 128:(ht + 1) * 128, j]

    w1t = np.ascontiguousarray(
        w1a.reshape(NEXP, NKD, 128, NFT, 128).transpose(0, 3, 2, 1, 4))
    w2t = np.ascontiguousarray(w2a.reshape(NEXP, FF, 2, 512).transpose(0, 2, 1, 3))
    b1t = np.ascontiguousarray(b1a.reshape(NEXP, NFT, 128).transpose(0, 2, 1))
    b2t = np.ascontiguousarray(np.broadcast_to(b2a[:, None, :], (NEXP, 128, DM)))
    tokt = np.ascontiguousarray(
        np.ascontiguousarray(topi.T).reshape(NEXP, NCT, 128).transpose(0, 2, 1)
        .astype(np.int32))
    cwt = cmbw

    nc1 = build_mega(nht=nht, K2=K2)
    maps = []
    for c in range(NCORE):
        maps.append(dict(
            xin=np.ascontiguousarray(hs[2 * c:2 * c + 2]),
            win_t=win_t, wout_t=wout_t, xpw_t=xpw_t, wdt_t=wdt_t,
            convw=convw, convb=convb, alog=alog, dpar=dpar, dtbv=dtbv,
            sels=sels, ident=ident,
            w1f=w1t, w2f=w2t, b1e=b1t, b2e=b2t, toka=tokt, cwa=cwt, gidx=gidx,
        ))
    r1 = run_bass_kernel_spmd(nc1, maps, core_ids=list(range(NCORE)))

    out = np.empty((NB, LSEQ, DM), dtype=np.float32)
    out[0:2] = np.asarray(r1.results[0]["final01"]).reshape(2, LSEQ, DM)
    for c in range(1, NCORE):
        out[2 * c:2 * c + 2] = np.asarray(r1.results[c]["resid"])
    return out
